# version 01
# baseline (speedup 1.0000x reference)
import numpy as np
import ml_dtypes
from contextlib import ExitStack

import concourse.bass as bass
import concourse.mybir as mybir
from concourse.bass_utils import run_bass_kernel_spmd

F32 = mybir.dt.float32
BF16 = mybir.dt.bfloat16
AF = mybir.ActivationFunctionType
ALU = mybir.AluOpType
AX = mybir.AxisListType

B, T, C = 4, 4096, 4096
H, G, D = 32, 8, 128
CH = 512
TOK = 2048
NFI = C // 128
ROLE_CHUNKS = {0: [0, 3, 4, 7], 1: [1, 2, 5, 6]}
CCOUNT = [2, 4, 6, 8]
RSQD = 1.0 / np.sqrt(128.0)
EXP_SHIFT = -8.0

DBG = {"stage": None}

ENGS = ("sync", "tensor", "scalar", "vector", "gpsimd")


class Sem:
    def __init__(self, h):
        self.h = h
        self.n = 0


class Plan:

    def __init__(self, block):
        self.block = block
        self.ops = {e: [] for e in ENGS}

    def do(self, eng, fn):
        self.ops[eng].append(fn)

    def op(self, eng, meth, *a, inc=None, inck=1, **kw):
        if inc is not None:
            inc.n += inck
            s, k = inc.h, inck
            self.ops[eng].append(
                lambda e, m=meth, a=a, kw=kw, s=s, k=k: getattr(e, m)(*a, **kw).then_inc(s, k))
        else:
            self.ops[eng].append(
                lambda e, m=meth, a=a, kw=kw: getattr(e, m)(*a, **kw))

    def w(self, eng, sem, val):
        if val > 0:
            self.ops[eng].append(lambda e, s=sem.h, v=val: e.wait_ge(s, v))

    def flush(self):
        for eng in ENGS:
            fns = self.ops[eng]
            if fns:
                getattr(self.block, eng)(lambda e, fs=tuple(fns): [f(e) for f in fs])
            self.ops[eng] = []


def barrier(p, sems):
    sb = sems["bar"]
    for k in ("in", "ld0", "ld1", "ld2", "ld3"):
        p.w("sync", sems[k], sems[k].n)
    for k in ("gi", "gl0", "gl1"):
        p.w("gpsimd", sems[k], sems[k].n)
    for k in ("st", "sc0", "sc1", "sc2", "sc3"):
        p.w("scalar", sems[k], sems[k].n)
    for eng in ENGS:
        sb.n += 1
        p.do(eng, lambda e, s=sb.h: e.nop().then_inc(s, 1))
    tgt = sb.n
    for eng in ENGS:
        p.w(eng, sb, tgt)


def build_program():
    nc = bass.Bass("TRN2", target_bir_lowering=False)

    def inp(name, shp):
        return nc.dram_tensor(name, shp, F32, kind="ExternalInput").ap()

    xT_own = inp("xT_own", [C, TOK])
    xT_full = inp("xT_full", [C, T])
    wq = inp("wq", [C, C])
    wk = inp("wk", [C, G * D])
    wv = inp("wv", [C, G * D])
    wo = inp("wo", [C, C])
    bo_t = inp("bo_t", [128, NFI])
    worgT = inp("worgT", [C, C])
    down = inp("down", [C, 512])
    up = inp("up", [512, C])
    wqtf = inp("wqtf", [C, 128])
    keysT = inp("keysT", [128, 8])
    i8 = inp("i8", [8, 8])
    ind = inp("ind", [8, 512])
    cosk = inp("cosk", [64, T])
    sink = inp("sink", [64, T])
    cosq = inp("cosq", [64, TOK])
    sinq = inp("sinq", [64, TOK])
    maskf = inp("maskf", [128, 4 * 2 * 4 * CH])
    cst = inp("cst", [128, 8])
    OUT = nc.dram_tensor("OUT", [C, TOK], F32, kind="ExternalOutput").ap()

    ksc = nc.dram_tensor("ksc", [G * D, T], BF16, kind="Internal").ap()
    vsc = nc.dram_tensor("vsc", [T, G * D], BF16, kind="Internal").ap()
    qsc = nc.dram_tensor("qsc", [C, TOK], BF16, kind="Internal").ap()
    asc = nc.dram_tensor("asc", [C, TOK], BF16, kind="Internal").ap()
    rsc = nc.dram_tensor("rsc", [C, TOK], F32, kind="Internal").ap()
    ysc = nc.dram_tensor("ysc", [C, TOK], F32, kind="Internal").ap()
    ccin = nc.dram_tensor("ccin", [1, 128], F32, kind="Internal")
    ccout = nc.dram_tensor("ccout", [2, 128], F32, kind="Internal")

    dbg_outs = {}
    stage = DBG["stage"]
    if stage is not None:
        for nm, src, shp, dt in [
            ("dbg_k", ksc, [G * D, T], BF16), ("dbg_v", vsc, [T, G * D], BF16),
            ("dbg_q", qsc, [C, TOK], BF16), ("dbg_a", asc, [C, TOK], BF16),
            ("dbg_r", rsc, [C, TOK], F32), ("dbg_y", ysc, [C, TOK], F32),
        ]:
            dbg_outs[nm] = (nc.dram_tensor(nm, shp, dt, kind="ExternalOutput").ap(), src)

    with ExitStack() as top:
        ec = top.enter_context
        sems = {k: Sem(ec(nc.semaphore("s_" + k)))
                for k in ("bar", "in", "gi", "pe", "ac", "dv", "st", "cc",
                          "ld0", "ld1", "ld2", "ld3",
                          "gl0", "gl1", "sc0", "sc1", "sc2", "sc3")}
        s_in, s_gi = sems["in"], sems["gi"]
        s_pe, s_ac, s_dv, s_st = sems["pe"], sems["ac"], sems["dv"], sems["st"]
        LD = [sems["ld%d" % i] for i in range(4)]
        GL = [sems["gl%d" % i] for i in range(2)]
        SC = [sems["sc%d" % i] for i in range(4)]

        cst_b = ec(nc.sbuf_tensor("cst_b", [128, 8], F32))
        bo_b = ec(nc.sbuf_tensor("bo_b", [128, NFI], F32))
        ones_c = ec(nc.sbuf_tensor("ones_c", [128, 1], BF16))
        ones_r = ec(nc.sbuf_tensor("ones_r", [1, 128], BF16))
        ones_1 = ec(nc.sbuf_tensor("ones_1", [1, 1], BF16))
        i8_b = ec(nc.sbuf_tensor("i8_b", [8, 8], BF16))
        ind_b = ec(nc.sbuf_tensor("ind_b", [8, 512], BF16))
        keysT_b = ec(nc.sbuf_tensor("keysT_b", [128, 8], BF16))
        wqtf_b = ec(nc.sbuf_tensor("wqtf_b", [128, NFI * 128], BF16))
        ysum = ec(nc.sbuf_tensor("ysum", [128, NFI * 4], F32))
        ssum = ec(nc.sbuf_tensor("ssum", [1, TOK], F32))
        ssq = ec(nc.sbuf_tensor("ssq", [1, TOK], F32))
        c_bc = ec(nc.sbuf_tensor("c_bc", [128, 4], F32))
        block = ec(nc.Block())
        p = Plan(block)

        p.op("sync", "dma_start", inc=s_in, inck=16, out=cst_b[:, :], in_=cst)
        p.op("sync", "dma_start", inc=s_in, inck=16, out=bo_b[:, :], in_=bo_t)
        p.op("gpsimd", "dma_start", inc=s_gi, inck=16, out=i8_b[:, :], in_=i8)
        p.op("gpsimd", "dma_start", inc=s_gi, inck=16, out=ind_b[:, :], in_=ind)
        p.op("gpsimd", "dma_start", inc=s_gi, inck=16, out=keysT_b[:, :], in_=keysT)
        p.op("gpsimd", "dma_start", inc=s_gi, inck=16,
             out=wqtf_b[:, :].rearrange("p (c k) -> p c k", c=NFI),
             in_=wqtf.rearrange("(c p) k -> p c k", p=128))
        p.op("vector", "memset", ones_c[:, :], 1.0)
        p.op("vector", "memset", ones_r[:, :], 1.0)
        p.op("vector", "memset", ones_1[:, :], 1.0)
        p.op("vector", "memset", ysum[:, :], 0.0)
        p.op("vector", "memset", ssum[:, :], 0.0)
        p.op("vector", "memset", ssq[:, :], 0.0)
        barrier(p, sems)
        p.flush()

        with ExitStack() as stA:
            eA = stA.enter_context

            evb = [eA(nc.sbuf_tensor(f"evb{i}", [128, 512], BF16)) for i in range(4)]
            rt_a = eA(nc.sbuf_tensor("rt_a", [64, 512], F32))
            rt_b2 = eA(nc.sbuf_tensor("rt_b2", [64, 512], F32))
            ps = [eA(nc.psum_tensor(f"ps{i}", [128, 512], F32)) for i in range(2)]
            stVK = ExitStack()
            eVK = stVK.enter_context
            w_big = eVK(nc.sbuf_tensor("w_big", [128, NFI * 1024], BF16))
            xb = [eVK(nc.sbuf_tensor(f"xb{i}", [128, NFI * CH], BF16)) for i in range(2)]
            cos_b = eVK(nc.sbuf_tensor("cos_b", [64, T], F32))
            sin_b = eVK(nc.sbuf_tensor("sin_b", [64, T], F32))

            p.op("sync", "dma_start", inc=s_in, inck=16, out=cos_b[:, :], in_=cosk)
            p.op("sync", "dma_start", inc=s_in, inck=16, out=sin_b[:, :], in_=sink)

            def stream_proj(kind):
                w_in = wv if kind == "v" else wk
                p.op("gpsimd", "dma_start", inc=s_gi, inck=16,
                     out=w_big[:, :].rearrange("p (c v) -> p c v", c=NFI),
                     in_=w_in.rearrange("(c p) v -> p c v", p=128))
                w_mark = s_gi.n
                x_marks = {}
                grp_done = {}
                ev_free = {}
                ps_free = {}
                tile_i = 0
                for tg in range(8):
                    if tg >= 2:
                        p.w("gpsimd", s_pe, grp_done[(tg - 2, "all")])
                    p.op("gpsimd", "dma_start", inc=GL[tg % 2], inck=16,
                         out=xb[tg % 2][:, :].rearrange("p (c t) -> p c t", c=NFI),
                         in_=xT_full[:, tg * CH:(tg + 1) * CH]
                         .rearrange("(c p) t -> p c t", p=128))
                    x_marks[tg] = GL[tg % 2].n
                    p.w("tensor", s_gi, w_mark)
                    p.w("tensor", GL[tg % 2], x_marks[tg])
                    ntile = 8
                    for j in range(ntile):
                        rot = tile_i % 2
                        if tile_i >= 2:
                            p.w("tensor", s_ac if kind == "v" else s_dv, ps_free[rot])
                        if kind == "v":
                            tsub, vh = j // 2, j % 2
                            for c in range(NFI):
                                p.op("tensor", "matmul", ps[rot][:, :],
                                     xb[tg % 2][:, c * CH + tsub * 128: c * CH + (tsub + 1) * 128],
                                     w_big[:, c * 1024 + vh * 512: c * 1024 + (vh + 1) * 512],
                                     start=(c == 0), stop=(c == NFI - 1),
                                     inc=s_pe if c == NFI - 1 else None)
                        else:
                            g = j
                            for c in range(NFI):
                                p.op("tensor", "matmul", ps[rot][:, :],
                                     w_big[:, c * 1024 + g * 128: c * 1024 + (g + 1) * 128],
                                     xb[tg % 2][:, c * CH:(c + 1) * CH],
                                     start=(c == 0), stop=(c == NFI - 1),
                                     inc=s_pe if c == NFI - 1 else None)
                        mm_mark = s_pe.n
                        erot = tile_i % 4
                        if tile_i >= 4:
                            eng = "scalar" if kind == "v" else "vector"
                            p.w(eng, SC[erot], ev_free[erot])
                        if kind == "v":
                            p.w("scalar", s_pe, mm_mark)
                            p.op("scalar", "copy", evb[erot][:, :], ps[rot][:, :], inc=s_ac)
                            ps_free[rot] = s_ac.n
                            p.w("scalar", s_ac, s_ac.n)
                            p.op("scalar", "dma_start", inc=SC[erot], inck=16,
                                 out=vsc[tg * CH + tsub * 128: tg * CH + (tsub + 1) * 128,
                                         vh * 512:(vh + 1) * 512],
                                 in_=evb[erot][:, :])
                        else:
                            p.w("vector", s_pe, mm_mark)
                            cs_ = cos_b[:, tg * CH:(tg + 1) * CH]
                            sn_ = sin_b[:, tg * CH:(tg + 1) * CH]
                            pr, pi = ps[rot][0:64, :], ps[rot][64:128, :]
                            p.op("vector", "tensor_mul", rt_a[:, :], pr, cs_)
                            p.op("vector", "tensor_mul", rt_b2[:, :], pi, sn_)
                            p.op("vector", "tensor_sub", evb[erot][0:64, :], rt_a[:, :], rt_b2[:, :])
                            p.op("vector", "tensor_mul", rt_a[:, :], pr, sn_)
                            p.op("vector", "tensor_mul", rt_b2[:, :], pi, cs_)
                            p.op("vector", "tensor_add", evb[erot][64:128, :],
                                 rt_a[:, :], rt_b2[:, :], inc=s_dv)
                            ps_free[rot] = s_dv.n
                            p.w("scalar", s_dv, s_dv.n)
                            p.op("scalar", "dma_start", inc=SC[erot], inck=16,
                                 out=ksc[g * 128:(g + 1) * 128, tg * CH:(tg + 1) * CH],
                                 in_=evb[erot][:, :])
                        ev_free[erot] = SC[erot].n
                        tile_i += 1
                    grp_done[(tg, "all")] = s_pe.n

            p.w("tensor", s_in, s_in.n)
            stream_proj("v")
            barrier(p, sems)
            stream_proj("k")
            barrier(p, sems)
            p.flush()
            stVK.close()

            with ExitStack() as stQ:
                eQ = stQ.enter_context
                xob = eQ(nc.sbuf_tensor("xob", [128, NFI * TOK], BF16))
                wqb = [eQ(nc.sbuf_tensor(f"wqb{i}", [128, NFI * 128], BF16)) for i in range(2)]
                cq_b = eQ(nc.sbuf_tensor("cq_b", [64, TOK], F32))
                sq_b = eQ(nc.sbuf_tensor("sq_b", [64, TOK], F32))

                p.op("sync", "dma_start", inc=s_in, inck=16, out=cq_b[:, :], in_=cosq)
                p.op("sync", "dma_start", inc=s_in, inck=16, out=sq_b[:, :], in_=sinq)
                for to in range(4):
                    p.op("gpsimd", "dma_start", inc=s_gi, inck=16,
                         out=xob[:, :].rearrange("p (c t) -> p c t", c=NFI)[:, :, to * CH:(to + 1) * CH],
                         in_=xT_own[:, to * CH:(to + 1) * CH].rearrange("(c p) t -> p c t", p=128))
                xo_mark = s_gi.n
                p.w("tensor", s_gi, xo_mark)
                p.w("tensor", s_in, s_in.n)
                wq_marks = {}
                hd_done = {}
                ev_free = {}
                ps_free = {}
                tile_i = 0
                for hd in range(H):
                    if hd >= 2:
                        p.w("gpsimd", s_pe, hd_done[hd - 2])
                    p.op("gpsimd", "dma_start", inc=GL[hd % 2], inck=16,
                         out=wqb[hd % 2][:, :].rearrange("p (c k) -> p c k", c=NFI),
                         in_=wq[:, hd * 128:(hd + 1) * 128].rearrange("(c p) k -> p c k", p=128))
                    wq_marks[hd] = GL[hd % 2].n
                    p.w("tensor", GL[hd % 2], wq_marks[hd])
                    for to in range(4):
                        rot = tile_i % 2
                        if tile_i >= 2:
                            p.w("tensor", s_dv, ps_free[rot])
                        for c in range(NFI):
                            p.op("tensor", "matmul", ps[rot][:, :],
                                 wqb[hd % 2][:, c * 128:(c + 1) * 128],
                                 xob[:, c * TOK + to * CH: c * TOK + (to + 1) * CH],
                                 start=(c == 0), stop=(c == NFI - 1),
                                 inc=s_pe if c == NFI - 1 else None)
                        mm_mark = s_pe.n
                        erot = tile_i % 4
                        if tile_i >= 4:
                            p.w("vector", SC[erot], ev_free[erot])
                        p.w("vector", s_pe, mm_mark)
                        cs_ = cq_b[:, to * CH:(to + 1) * CH]
                        sn_ = sq_b[:, to * CH:(to + 1) * CH]
                        pr, pi = ps[rot][0:64, :], ps[rot][64:128, :]
                        p.op("vector", "tensor_mul", rt_a[:, :], pr, cs_)
                        p.op("vector", "tensor_mul", rt_b2[:, :], pi, sn_)
                        p.op("vector", "tensor_sub", evb[erot][0:64, :], rt_a[:, :], rt_b2[:, :])
                        p.op("vector", "tensor_mul", rt_a[:, :], pr, sn_)
                        p.op("vector", "tensor_mul", rt_b2[:, :], pi, cs_)
                        p.op("vector", "tensor_add", evb[erot][64:128, :],
                             rt_a[:, :], rt_b2[:, :], inc=s_dv)
                        ps_free[rot] = s_dv.n
                        p.w("scalar", s_dv, s_dv.n)
                        p.op("scalar", "dma_start", inc=SC[erot], inck=16,
                             out=qsc[hd * 128:(hd + 1) * 128, to * CH:(to + 1) * CH],
                             in_=evb[erot][:, :])
                        ev_free[erot] = SC[erot].n
                        tile_i += 1
                    hd_done[hd] = s_pe.n
                barrier(p, sems)
                p.flush()

        if stage in ("kv", "q"):
            emit_debug(p, sems, dbg_outs, stage)
            p.flush()
            return nc

        with ExitStack() as st4:
            e4 = st4.enter_context
            mask_b = e4(nc.sbuf_tensor("mask_b", [128, 4 * 2 * 4 * CH], BF16))
            kgb = [e4(nc.sbuf_tensor(f"kgb{i}", [128, T], BF16)) for i in range(2)]
            vgb = [e4(nc.sbuf_tensor(f"vgb{i}", [128, T], BF16)) for i in range(2)]
            qhb = [e4(nc.sbuf_tensor(f"qhb{i}", [128, TOK], BF16)) for i in range(2)]
            p_sb = [e4(nc.sbuf_tensor(f"p_sb{i}", [128, 1024], BF16)) for i in range(3)]
            rsum = e4(nc.sbuf_tensor("rsum", [1, 512], F32))
            rhi = e4(nc.sbuf_tensor("rhi", [1, 512], BF16))
            rhf = e4(nc.sbuf_tensor("rhf", [1, 512], F32))
            rlf = e4(nc.sbuf_tensor("rlf", [1, 512], F32))
            rlo = e4(nc.sbuf_tensor("rlo", [1, 512], BF16))
            rt_sb = [e4(nc.sbuf_tensor(f"rt_sb{i}", [128, 512], F32)) for i in range(2)]
            at_sb = [e4(nc.sbuf_tensor(f"at_sb{i}", [128, 512], BF16)) for i in range(2)]
            st_ps = e4(nc.psum_tensor("st_ps", [128, 2048], F32))
            pv_ps = e4(nc.psum_tensor("pv_ps", [128, 512], F32))
            sum_ps = e4(nc.psum_tensor("sum_ps", [128, 512], F32))
            rt_ps = e4(nc.psum_tensor("rt_ps", [128, 512], F32))

            p.op("gpsimd", "dma_start", inc=s_gi, inck=16, out=mask_b[:, :], in_=maskf)
            mask_mark = s_gi.n

            def load_kv(g, pe_wait):
                sem = LD[g % 2]
                p.w("sync", s_pe, pe_wait)
                p.op("sync", "dma_start", inc=sem, inck=16,
                     out=kgb[g % 2][:, :], in_=ksc[g * 128:(g + 1) * 128, :])
                vv = vsc[:, g * 128:(g + 1) * 128].rearrange("(ks p) d -> p ks d", p=128)
                p.op("sync", "dma_start", inc=sem, inck=16,
                     out=vgb[g % 2][:, :].rearrange("p (ks d) -> p ks d", ks=32)[:, 0:16, :],
                     in_=vv[:, 0:16, :])
                p.op("sync", "dma_start", inc=sem, inck=16,
                     out=vgb[g % 2][:, :].rearrange("p (ks d) -> p ks d", ks=32)[:, 16:32, :],
                     in_=vv[:, 16:32, :])
                return (sem, sem.n)

            def load_q(h, pe_wait):
                sem = LD[2 + h % 2]
                p.w("sync", s_pe, pe_wait)
                p.op("sync", "dma_start", inc=sem, inck=16,
                     out=qhb[h % 2][:, :], in_=qsc[h * 128:(h + 1) * 128, :])
                return (sem, sem.n)

            kv_mark = {}
            q_mark = {}
            g_done = {-1: 0, -2: 0}
            h_done = {-1: 0, -2: 0}
            kv_mark[0] = load_kv(0, 0)
            q_mark[0] = load_q(0, 0)
            exp_mark = {}
            pair_consumed = {}
            recip_mark = 0
            at_ev_mark = 0
            rt_cp_mark = 0
            at_st_mark = {0: 0, 1: 0}
            mask_dv_mark = {}

            slot_i = 0
            for g in range(G):
                if g + 1 < G:
                    kv_mark[g + 1] = load_kv(g + 1, g_done[g - 1])
                for hh in range(4):
                    h = g * 4 + hh
                    if h + 1 < H:
                        hq = h + 1
                        q_mark[hq] = load_q(hq, h_done[h - 1])
                    p.w("tensor", kv_mark[g][0], kv_mark[g][1])
                    p.w("tensor", q_mark[h][0], q_mark[h][1])
                    if h == 0:
                        p.w("vector", s_gi, mask_mark)
                    for s in range(4):
                        cs = CCOUNT[s]
                        npair = 2 * cs
                        qs = qhb[h % 2][:, s * CH:(s + 1) * CH]
                        first_sum = True
                        for pair in range(npair):
                            kk0 = 2 * pair
                            half = pair % 2
                            if pair >= 2:
                                p.w("tensor", s_ac, exp_mark[(slot_i, pair - 2)])
                            for dkk in range(2):
                                kk = kk0 + dkk
                                p.op("tensor", "matmul",
                                     st_ps[:, half * 1024 + dkk * 512: half * 1024 + (dkk + 1) * 512],
                                     kgb[g % 2][:, kk * 128:(kk + 1) * 128], qs,
                                     start=True, stop=True,
                                     inc=s_pe if dkk == 1 else None)
                            st_mark = s_pe.n
                            p.w("scalar", s_pe, st_mark)
                            if pair >= 3:
                                p.w("scalar", s_pe, pair_consumed[(slot_i, pair - 3)])
                            p.op("scalar", "activation",
                                 p_sb[pair % 3][:, :], st_ps[:, half * 1024: half * 1024 + 1024],
                                 AF.Exp, bias=cst_b[:, 0:1], scale=RSQD, inc=s_ac)
                            exp_mark[(slot_i, pair)] = s_ac.n
                            ck = kk0 // 4
                            masked = ck >= cs - 2
                            if masked:
                                which = ck - (cs - 2)
                                base = ((s * 2 + which) * 4 + (kk0 % 4)) * CH
                                p.w("vector", s_ac, exp_mark[(slot_i, pair)])
                                p.op("vector", "tensor_mul", p_sb[pair % 3][:, :],
                                     p_sb[pair % 3][:, :], mask_b[:, base: base + 1024],
                                     inc=s_dv)
                                mask_dv_mark[(slot_i, pair)] = s_dv.n
                            p.w("tensor", s_ac, exp_mark[(slot_i, pair)])
                            if masked:
                                p.w("tensor", s_dv, mask_dv_mark[(slot_i, pair)])
                            if first_sum:
                                p.w("tensor", s_dv, recip_mark)
                                p.w("tensor", s_dv, at_ev_mark)
                            for dkk in range(2):
                                kk = kk0 + dkk
                                pt = p_sb[pair % 3][:, dkk * 512:(dkk + 1) * 512]
                                p.op("tensor", "matmul", sum_ps[0:1, :], ones_c[:, :], pt,
                                     start=first_sum and dkk == 0,
                                     stop=(pair == npair - 1 and dkk == 1))
                                p.op("tensor", "matmul", pv_ps[:, :],
                                     vgb[g % 2][:, kk * 128:(kk + 1) * 128], pt,
                                     start=first_sum and dkk == 0,
                                     stop=(pair == npair - 1 and dkk == 1),
                                     inc=s_pe if (pair == npair - 1 and dkk == 1) else None)
                            first_sum = False
                            pair_consumed[(slot_i, pair)] = s_pe.n
                        pv_done = s_pe.n
                        p.w("vector", s_pe, pv_done)
                        p.op("vector", "reciprocal", rsum[:, :], sum_ps[0:1, :], inc=s_dv)
                        p.w("vector", s_dv, s_dv.n)
                        p.op("vector", "tensor_copy", rhi[:, :], rsum[:, :])
                        p.op("vector", "tensor_copy", rhf[:, :], rhi[:, :])
                        p.op("vector", "tensor_sub", rlf[:, :], rsum[:, :], rhf[:, :])
                        p.op("vector", "tensor_copy", rlo[:, :], rlf[:, :], inc=s_dv)
                        recip_mark = s_dv.n
                        hl_mark = s_dv.n
                        p.w("tensor", s_dv, hl_mark)
                        p.w("tensor", s_ac, rt_cp_mark)
                        p.op("tensor", "matmul", rt_ps[:, :], ones_r[:, :], rhi[:, :],
                             start=True, stop=False)
                        p.op("tensor", "matmul", rt_ps[:, :], ones_r[:, :], rlo[:, :],
                             start=False, stop=True, inc=s_pe)
                        rt_mm = s_pe.n
                        p.w("scalar", s_pe, rt_mm)
                        p.op("scalar", "copy", rt_sb[s % 2][:, :], rt_ps[:, :], inc=s_ac)
                        rt_cp_mark = s_ac.n
                        p.w("vector", s_ac, rt_cp_mark)
                        p.w("vector", SC[s % 2], at_st_mark[s % 2])
                        p.op("vector", "tensor_mul", at_sb[s % 2][:, :],
                             pv_ps[:, :], rt_sb[s % 2][:, :], inc=s_dv)
                        at_ev_mark = s_dv.n
                        p.w("scalar", s_dv, at_ev_mark)
                        p.op("scalar", "dma_start", inc=SC[s % 2], inck=16,
                             out=asc[h * 128:(h + 1) * 128, s * CH:(s + 1) * CH],
                             in_=at_sb[s % 2][:, :])
                        at_st_mark[s % 2] = SC[s % 2].n
                        slot_i += 1
                    h_done[h] = s_pe.n
                g_done[g] = s_pe.n
            barrier(p, sems)
            p.flush()

        if stage == "attn":
            emit_debug(p, sems, dbg_outs, stage)
            p.flush()
            return nc

        with ExitStack() as st5:
            e5 = st5.enter_context
            attn_b = e5(nc.sbuf_tensor("attn_b", [128, NFI * TOK], BF16))
            wob = [e5(nc.sbuf_tensor(f"wob{i}", [128, NFI * 128], BF16)) for i in range(2)]
            xo_f = [e5(nc.sbuf_tensor(f"xo_f{i}", [128, 512], F32)) for i in range(3)]
            r_f = [e5(nc.sbuf_tensor(f"r_f{i}", [128, 512], F32)) for i in range(3)]
            r_bb = [e5(nc.sbuf_tensor(f"r_bb{i}", [128, 512], BF16)) for i in range(2)]
            r2_bb = [e5(nc.sbuf_tensor(f"r2_bb{i}", [128, 512], BF16)) for i in range(2)]
            ps_wo = [e5(nc.psum_tensor(f"ps_wo{i}", [128, 512], F32)) for i in range(2)]
            st_s = e5(nc.psum_tensor("st_s", [128, 512], F32))
            st_q = e5(nc.psum_tensor("st_q", [128, 512], F32))

            p.op("sync", "dma_start", inc=s_in, inck=16,
                 out=attn_b[:, :].rearrange("p (hh q) -> p hh q", hh=NFI),
                 in_=asc.rearrange("(hh p) q -> p hh q", p=128))
            attn_mark = s_in.n
            emit_proj_res_stats(
                p, sems, nc, src_res=xT_own, res_scale_bias=True,
                w_dram=wo, w_bufs=wob, act_b=attn_b, xo_f=xo_f, r_f=r_f,
                r_bb=r_bb, r2_bb=r2_bb, ps_rot=ps_wo, st_s=st_s, st_q=st_q,
                ssum=ssum, ssq=ssq, bo_b=bo_b, out_dram=rsc,
                pre_marks=(attn_mark, 0), ones_c=ones_c, extra=None)
            barrier(p, sems)
            p.flush()

        if stage == "wo":
            emit_debug(p, sems, dbg_outs, stage)
            p.flush()
            return nc

        with ExitStack() as st6:
            e6 = st6.enter_context
            ps_a = [e6(nc.psum_tensor(f"ps_a{i}", [128, 512], F32)) for i in range(2)]
            stYH = ExitStack()
            st6.enter_context(stYH)
            y_b = stYH.enter_context(nc.sbuf_tensor("y_b", [128, NFI * TOK], BF16))

            with ExitStack() as st6i:
                e6i = st6i.enter_context
                mh = {}
                for nm in ("mh", "ml", "rh", "rl"):
                    mh[nm] = e6i(nc.sbuf_tensor("ln_" + nm, [1, TOK], BF16))
                tmp1 = e6i(nc.sbuf_tensor("ln_t1", [1, TOK], F32))
                tmp2 = e6i(nc.sbuf_tensor("ln_t2", [1, TOK], F32))
                M_sb = [e6i(nc.sbuf_tensor(f"M_sb{i}", [128, 512], F32)) for i in range(2)]
                R_sb = [e6i(nc.sbuf_tensor(f"R_sb{i}", [128, 512], F32)) for i in range(2)]
                rr_f = [e6i(nc.sbuf_tensor(f"rr_f{i}", [128, 512], F32)) for i in range(3)]
                yy_f = [e6i(nc.sbuf_tensor(f"yy_f{i}", [128, 512], F32)) for i in range(3)]
                ysp = e6i(nc.sbuf_tensor("ysp", [128, 1], F32))
                br_ps = e6i(nc.psum_tensor("br_ps", [128, 1024], F32))

                emit_ln_finalize(p, sems, ssum, ssq, tmp1, tmp2, mh)
                emit_ln_apply(p, sems, rsc, mh, tmp1, M_sb, R_sb, rr_f, yy_f,
                              br_ps, ones_r, out_dram=ysc, y_b=y_b, ysp=ysp, ysum=ysum,
                              out_dram2=None, OUT=None)
                barrier(p, sems)
                p.flush()

            if stage == "ln1":
                ys_o = nc.dram_tensor("dbg_ysum", [128, NFI * 4], F32,
                                      kind="ExternalOutput").ap()
                p.op("scalar", "dma_start", inc=s_st, inck=16,
                     out=ys_o, in_=ysum[:, :])
                p.w("scalar", s_st, s_st.n)
                emit_debug(p, sems, dbg_outs, stage)
                p.flush()
                return nc

            h_b = stYH.enter_context(nc.sbuf_tensor("h_b", [128, 4 * TOK], BF16))
            with ExitStack() as st7:
                e7 = st7.enter_context
                dnb = [e7(nc.sbuf_tensor(f"dnb{i}", [128, NFI * 128], BF16)) for i in range(2)]
                ysb = e7(nc.sbuf_tensor("cc_ysb", [128, NFI * 4], BF16))
                q4 = e7(nc.sbuf_tensor("cc_q4", [128, 4], F32))
                q_sb = e7(nc.sbuf_tensor("cc_qsb", [128, 1], F32))
                qa = e7(nc.sbuf_tensor("cc_qa", [1, 128], F32))
                qb = e7(nc.sbuf_tensor("cc_qb", [1, 128], F32))
                qt = e7(nc.sbuf_tensor("cc_qt", [1, 128], F32))
                qt_bf = e7(nc.sbuf_tensor("cc_qtb", [1, 128], BF16))
                qc_bf = e7(nc.sbuf_tensor("cc_qcb", [128, 1], BF16))
                sc_bf = e7(nc.sbuf_tensor("cc_scb", [8, 1], BF16))
                e8s = e7(nc.sbuf_tensor("cc_e8", [1, 8], F32))
                se = e7(nc.sbuf_tensor("cc_se", [1, 1], F32))
                rse = e7(nc.sbuf_tensor("cc_rse", [1, 1], F32))
                cf = e7(nc.sbuf_tensor("cc_cf", [1, 8], F32))
                cf_bf = e7(nc.sbuf_tensor("cc_cfb", [1, 8], BF16))
                ct_bf = e7(nc.sbuf_tensor("cc_ctb", [8, 1], BF16))

                emit_coeffs(p, sems, nc, ysum, wqtf_b, keysT_b, i8_b, ind_b,
                            ones_1, ccin, ccout,
                            (ysb, q4, q_sb, qa, qb, qt, qt_bf, qc_bf, sc_bf,
                             e8s, se, rse, cf, cf_bf, ct_bf), c_bc, ps_a)
                if stage == "ffn":
                    for nm, tile in [("dbg_qsb", q_sb), ("dbg_qa", qa),
                                     ("dbg_qb", qb), ("dbg_qt", qt),
                                     ("dbg_e8", e8s), ("dbg_cf", cf),
                                     ("dbg_cbc", c_bc), ("dbg_ysum", ysum)]:
                        shp = list(tile.shape)
                        ap_o = nc.dram_tensor(nm, shp, F32, kind="ExternalOutput").ap()
                        p.w("scalar", s_dv, s_dv.n)
                        p.op("scalar", "dma_start", inc=s_st, inck=16,
                             out=ap_o, in_=tile[:, :])
                dn_marks = {}
                ec_done = {}
                ps_free = {}
                tile_i = 0
                cbc_mark = s_dv.n
                for ecx in range(4):
                    if ecx >= 2:
                        p.w("gpsimd", s_pe, ec_done[ecx - 2])
                    p.op("gpsimd", "dma_start", inc=GL[ecx % 2], inck=16,
                         out=dnb[ecx % 2][:, :].rearrange("p (c k) -> p c k", c=NFI),
                         in_=down[:, ecx * 128:(ecx + 1) * 128].rearrange("(c p) k -> p c k", p=128))
                    dn_marks[ecx] = GL[ecx % 2].n
                    p.w("tensor", GL[ecx % 2], dn_marks[ecx])
                    for ts in range(4):
                        rot = tile_i % 2
                        if tile_i >= 2:
                            p.w("tensor", s_ac, ps_free[rot])
                        for c in range(NFI):
                            p.op("tensor", "matmul", ps_a[rot][:, :],
                                 dnb[ecx % 2][:, c * 128:(c + 1) * 128],
                                 y_b[:, c * TOK + ts * CH: c * TOK + (ts + 1) * CH],
                                 start=(c == 0), stop=(c == NFI - 1),
                                 inc=s_pe if c == NFI - 1 else None)
                        p.w("scalar", s_pe, s_pe.n)
                        p.w("scalar", s_dv, cbc_mark)
                        p.op("scalar", "activation",
                             h_b[:, ecx * TOK + ts * CH: ecx * TOK + (ts + 1) * CH],
                             ps_a[rot][:, :], AF.Copy, scale=c_bc[:, ecx:ecx + 1], inc=s_ac)
                        ps_free[rot] = s_ac.n
                        tile_i += 1
                    ec_done[ecx] = s_pe.n
                barrier(p, sems)
                p.flush()

            p.op("vector", "memset", ssum[:, :], 0.0)
            p.op("vector", "memset", ssq[:, :], 0.0)
            with ExitStack() as st8:
                e8 = st8.enter_context
                wgb = [e8(nc.sbuf_tensor(f"wgb{i}", [128, NFI * 128], BF16)) for i in range(2)]
                upb = [e8(nc.sbuf_tensor(f"upb{i}", [128, 4 * 128], BF16)) for i in range(2)]
                yo_f = [e8(nc.sbuf_tensor(f"yo_f{i}", [128, 512], F32)) for i in range(3)]
                r_f = [e8(nc.sbuf_tensor(f"r8_f{i}", [128, 512], F32)) for i in range(3)]
                r_bb = [e8(nc.sbuf_tensor(f"r8_bb{i}", [128, 512], BF16)) for i in range(2)]
                r2_bb = [e8(nc.sbuf_tensor(f"r8_2bb{i}", [128, 512], BF16)) for i in range(2)]
                st_s = e8(nc.psum_tensor("st8_s", [128, 512], F32))
                st_q = e8(nc.psum_tensor("st8_q", [128, 512], F32))

                emit_proj_res_stats(
                    p, sems, nc, src_res=ysc, res_scale_bias=False,
                    w_dram=worgT, w_bufs=wgb, act_b=y_b, xo_f=yo_f, r_f=r_f,
                    r_bb=r_bb, r2_bb=r2_bb, ps_rot=ps_a, st_s=st_s, st_q=st_q,
                    ssum=ssum, ssq=ssq, bo_b=None, out_dram=rsc,
                    pre_marks=(0, 0), ones_c=ones_c,
                    extra={"up": up, "upb": upb, "h_b": h_b})
                barrier(p, sems)
                p.flush()
            stYH.close()

            with ExitStack() as st9:
                e9 = st9.enter_context
                mh = {}
                for nm in ("mh", "ml", "rh", "rl"):
                    mh[nm] = e9(nc.sbuf_tensor("l2_" + nm, [1, TOK], BF16))
                tmp1 = e9(nc.sbuf_tensor("l2_t1", [1, TOK], F32))
                tmp2 = e9(nc.sbuf_tensor("l2_t2", [1, TOK], F32))
                M_sb = [e9(nc.sbuf_tensor(f"M2_sb{i}", [128, 512], F32)) for i in range(2)]
                R_sb = [e9(nc.sbuf_tensor(f"R2_sb{i}", [128, 512], F32)) for i in range(2)]
                rr_f = [e9(nc.sbuf_tensor(f"r9_f{i}", [128, 512], F32)) for i in range(3)]
                yy_f = [e9(nc.sbuf_tensor(f"y9_f{i}", [128, 512], F32)) for i in range(3)]
                br_ps = e9(nc.psum_tensor("br2_ps", [128, 1024], F32))

                emit_ln_finalize(p, sems, ssum, ssq, tmp1, tmp2, mh)
                emit_ln_apply(p, sems, rsc, mh, tmp1, M_sb, R_sb, rr_f, yy_f,
                              br_ps, ones_r, out_dram=None, y_b=None, ysp=None,
                              ysum=None, out_dram2=None, OUT=OUT)
                for k in ("st", "sc0", "sc1", "sc2", "sc3"):
                    p.w("scalar", sems[k], sems[k].n)
                if stage == "ffn":
                    emit_debug(p, sems, dbg_outs, stage)
                p.flush()
    return nc


def emit_debug(p, sems, dbg_outs, stage):
    s_gi = sems["gi"]
    for nm, (out_ap, src) in dbg_outs.items():
        p.op("gpsimd", "dma_start", inc=s_gi, inck=16, out=out_ap, in_=src)
    p.w("gpsimd", s_gi, s_gi.n)


def emit_proj_res_stats(p, sems, nc, *, src_res, res_scale_bias, w_dram, w_bufs,
                        act_b, xo_f, r_f, r_bb, r2_bb, ps_rot, st_s, st_q,
                        ssum, ssq, bo_b, out_dram, pre_marks, ones_c, extra):
    s_in, s_gi = sems["in"], sems["gi"]
    s_pe, s_ac, s_dv, s_st = sems["pe"], sems["ac"], sems["dv"], sems["st"]
    LD = [sems["ld%d" % i] for i in range(4)]
    GL = [sems["gl%d" % i] for i in range(2)]
    SC = [sems["sc%d" % i] for i in range(4)]
    act_mark, _ = pre_marks
    w_marks = {}
    co_done = {}
    ps_free = {}
    xo_marks = {}
    r_free = {}
    stat_dv = 0
    tile_i = 0
    p.w("tensor", s_in, act_mark)
    for co in range(NFI):
        if co >= 2:
            p.w("gpsimd", s_pe, co_done[co - 2])
        p.op("gpsimd", "dma_start", inc=GL[co % 2], inck=16,
             out=w_bufs[co % 2][:, :].rearrange("p (c k) -> p c k", c=NFI),
             in_=w_dram[:, co * 128:(co + 1) * 128].rearrange("(c p) k -> p c k", p=128))
        if extra is not None:
            p.op("gpsimd", "dma_start", inc=GL[co % 2], inck=16,
                 out=extra["upb"][co % 2][:, :].rearrange("p (e k) -> p e k", e=4),
                 in_=extra["up"][:, co * 128:(co + 1) * 128].rearrange("(e p) k -> p e k", p=128))
        w_marks[co] = GL[co % 2].n
        p.w("tensor", GL[co % 2], w_marks[co])
        for ts in range(4):
            rot = tile_i % 2
            xrot = tile_i % 3
            if tile_i >= 3:
                p.w("sync", s_dv, r_free.get(("x", xrot), 0))
            p.op("sync", "dma_start", inc=LD[xrot], inck=16,
                 out=xo_f[xrot][:, :],
                 in_=src_res[co * 128:(co + 1) * 128, ts * CH:(ts + 1) * CH])
            xo_marks[tile_i] = LD[xrot].n
            if tile_i >= 2:
                p.w("tensor", s_dv, ps_free[rot])
            for c in range(NFI):
                p.op("tensor", "matmul", ps_rot[rot][:, :],
                     w_bufs[co % 2][:, c * 128:(c + 1) * 128],
                     act_b[:, c * TOK + ts * CH: c * TOK + (ts + 1) * CH],
                     start=(c == 0), stop=(extra is None and c == NFI - 1),
                     inc=(s_pe if (extra is None and c == NFI - 1) else None))
            if extra is not None:
                for ecx in range(4):
                    p.op("tensor", "matmul", ps_rot[rot][:, :],
                         extra["upb"][co % 2][:, ecx * 128:(ecx + 1) * 128],
                         extra["h_b"][:, ecx * TOK + ts * CH: ecx * TOK + (ts + 1) * CH],
                         start=False, stop=(ecx == 3),
                         inc=s_pe if ecx == 3 else None)
            mm_mark = s_pe.n
            p.w("vector", s_pe, mm_mark)
            p.w("vector", LD[xrot], xo_marks[tile_i])
            rrot = tile_i % 3
            if tile_i >= 3:
                p.w("vector", SC[rrot], r_free.get(("r", rrot), 0))
            if res_scale_bias:
                p.op("vector", "scalar_tensor_tensor", r_f[rrot][:, :],
                     ps_rot[rot][:, :], bo_b[:, co:co + 1], xo_f[xrot][:, :],
                     ALU.add, ALU.add)
            else:
                p.op("vector", "tensor_add", r_f[rrot][:, :],
                     ps_rot[rot][:, :], xo_f[xrot][:, :])
            brot = tile_i % 2
            p.op("vector", "tensor_copy", r_bb[brot][:, :], r_f[rrot][:, :], inc=s_dv)
            ps_free[rot] = s_dv.n
            r_free[("x", xrot)] = s_dv.n
            rb_mark = s_dv.n
            p.w("scalar", s_dv, rb_mark)
            p.op("scalar", "square", r2_bb[brot][:, :], r_bb[brot][:, :], inc=s_ac)
            sq_mark = s_ac.n
            p.w("tensor", s_dv, stat_dv)
            p.w("tensor", s_dv, rb_mark)
            p.w("tensor", s_ac, sq_mark)
            p.op("tensor", "matmul", st_s[0:1, :], ones_c[:, :], r_bb[brot][:, :],
                 start=True, stop=True)
            p.op("tensor", "matmul", st_q[0:1, :], ones_c[:, :], r2_bb[brot][:, :],
                 start=True, stop=True, inc=s_pe)
            stat_pe = s_pe.n
            p.w("vector", s_pe, stat_pe)
            p.op("vector", "tensor_add", ssum[0:1, ts * CH:(ts + 1) * CH],
                 ssum[0:1, ts * CH:(ts + 1) * CH], st_s[0:1, :])
            p.op("vector", "tensor_add", ssq[0:1, ts * CH:(ts + 1) * CH],
                 ssq[0:1, ts * CH:(ts + 1) * CH], st_q[0:1, :], inc=s_dv)
            stat_dv = s_dv.n
            p.w("scalar", s_dv, rb_mark)
            p.op("scalar", "dma_start", inc=SC[rrot], inck=16,
                 out=out_dram[co * 128:(co + 1) * 128, ts * CH:(ts + 1) * CH],
                 in_=r_f[rrot][:, :])
            r_free[("r", rrot)] = SC[rrot].n
            tile_i += 1
        co_done[co] = s_pe.n


def emit_ln_finalize(p, sems, ssum, ssq, tmp1, tmp2, mh):
    s_ac, s_dv = sems["ac"], sems["dv"]
    inv = 1.0 / C
    p.op("vector", "tensor_scalar", tmp1[:, :], ssum[:, :], inv, None, ALU.mult)
    p.op("vector", "tensor_copy", mh["mh"][:, :], tmp1[:, :])
    p.op("vector", "tensor_copy", tmp2[:, :], mh["mh"][:, :])
    p.op("vector", "tensor_sub", tmp2[:, :], tmp1[:, :], tmp2[:, :])
    p.op("vector", "tensor_copy", mh["ml"][:, :], tmp2[:, :])
    p.op("vector", "tensor_mul", tmp2[:, :], tmp1[:, :], tmp1[:, :])
    p.op("vector", "tensor_scalar", tmp1[:, :], ssq[:, :], inv, None, ALU.mult)
    p.op("vector", "tensor_sub", tmp1[:, :], tmp1[:, :], tmp2[:, :])
    p.op("vector", "tensor_scalar", tmp1[:, :], tmp1[:, :], 1e-5, None, ALU.add,
         inc=s_dv)
    vm = s_dv.n
    p.w("scalar", s_dv, vm)
    p.op("scalar", "sqrt", tmp2[:, :], tmp1[:, :], inc=s_ac)
    p.w("vector", s_ac, s_ac.n)
    p.op("vector", "reciprocal", tmp1[:, :], tmp2[:, :], inc=s_dv)
    p.w("vector", s_dv, s_dv.n)
    p.op("vector", "tensor_copy", mh["rh"][:, :], tmp1[:, :])
    p.op("vector", "tensor_copy", tmp2[:, :], mh["rh"][:, :])
    p.op("vector", "tensor_sub", tmp2[:, :], tmp1[:, :], tmp2[:, :])
    p.op("vector", "tensor_copy", mh["rl"][:, :], tmp2[:, :], inc=s_dv)


def emit_ln_apply(p, sems, r_dram, mh, scratch_row, M_sb, R_sb, rr_f, yy_f,
                  br_ps, ones_r, *, out_dram, y_b, ysp, ysum, out_dram2, OUT):
    s_in, s_pe = sems["in"], sems["pe"]
    s_ac, s_dv, s_st = sems["ac"], sems["dv"], sems["st"]
    LD = [sems["ld%d" % i] for i in range(4)]
    SC = [sems["sc%d" % i] for i in range(4)]
    hl_mark = s_dv.n
    r_marks = {}
    buf_free = {}
    mr_used = {}
    mr_mark = {}
    tile_i = 0
    for ts in range(4):
        p.w("tensor", s_dv, hl_mark)
        p.w("tensor", s_ac, mr_mark.get(ts - 1, 0))
        sl = slice(ts * CH, (ts + 1) * CH)
        p.op("tensor", "matmul", br_ps[:, 0:512], ones_r[:, :], mh["mh"][0:1, sl],
             start=True, stop=False)
        p.op("tensor", "matmul", br_ps[:, 0:512], ones_r[:, :], mh["ml"][0:1, sl],
             start=False, stop=True)
        p.op("tensor", "matmul", br_ps[:, 512:1024], ones_r[:, :], mh["rh"][0:1, sl],
             start=True, stop=False)
        p.op("tensor", "matmul", br_ps[:, 512:1024], ones_r[:, :], mh["rl"][0:1, sl],
             start=False, stop=True, inc=s_pe)
        br_mark = s_pe.n
        p.w("scalar", s_pe, br_mark)
        p.w("scalar", s_dv, mr_used.get(ts - 2, 0))
        p.op("scalar", "copy", M_sb[ts % 2][:, :], br_ps[:, 0:512])
        p.op("scalar", "copy", R_sb[ts % 2][:, :], br_ps[:, 512:1024], inc=s_ac)
        mr_mark[ts] = s_ac.n
        p.w("vector", s_ac, mr_mark[ts])
        for fi in range(NFI):
            rrot = tile_i % 3
            if tile_i >= 3:
                p.w("sync", s_dv, buf_free.get(("rr", rrot), 0))
            p.op("sync", "dma_start", inc=LD[rrot], inck=16,
                 out=rr_f[rrot][:, :], in_=r_dram[fi * 128:(fi + 1) * 128, sl])
            r_marks[tile_i] = LD[rrot].n
            p.w("vector", LD[rrot], r_marks[tile_i])
            yrot = tile_i % 3
            if tile_i >= 3:
                p.w("vector", SC[yrot], buf_free.get(("yy", yrot), 0))
            p.op("vector", "tensor_sub", yy_f[yrot][:, :], rr_f[rrot][:, :],
                 M_sb[ts % 2][:, :])
            p.op("vector", "tensor_mul", yy_f[yrot][:, :], yy_f[yrot][:, :],
                 R_sb[ts % 2][:, :], inc=s_dv)
            buf_free[("rr", rrot)] = s_dv.n
            y_mark = s_dv.n
            if y_b is not None:
                p.op("vector", "tensor_copy",
                     y_b[:, fi * TOK + ts * CH: fi * TOK + (ts + 1) * CH],
                     yy_f[yrot][:, :])
                p.op("vector", "tensor_reduce", ysum[:, fi * 4 + ts: fi * 4 + ts + 1],
                     yy_f[yrot][:, :], AX.X, ALU.add, inc=s_dv)
                y_mark = s_dv.n
            p.w("scalar", s_dv, y_mark)
            dst = OUT if OUT is not None else out_dram
            p.op("scalar", "dma_start", inc=SC[yrot], inck=16,
                 out=dst[fi * 128:(fi + 1) * 128, sl], in_=yy_f[yrot][:, :])
            buf_free[("yy", yrot)] = SC[yrot].n
            tile_i += 1
        mr_used[ts] = s_dv.n


def emit_coeffs(p, sems, nc, ysum, wqtf_b, keysT_b, i8_b, ind_b, ones_1,
                ccin, ccout, tiles, c_bc, ps_a):
    s_in = sems["in"]
    s_pe, s_ac, s_dv, s_cc = sems["pe"], sems["ac"], sems["dv"], sems["cc"]
    ysb, q4, q_sb, qa, qb, qt, qt_bf, qc_bf, sc_bf, e8, se, rse, cf, cf_bf, ct_bf = tiles
    qps = ps_a[0][:, 0:4]
    qcps = ps_a[1][:, 0:1]
    scps = ps_a[0][0:8, 0:1]
    sctps = ps_a[1][0:1, 0:8]
    ctps = ps_a[0][0:8, 0:1]
    cbps = ps_a[1][:, 0:1]
    p.op("vector", "tensor_copy", ysb[:, :], ysum[:, :], inc=s_dv)
    m = s_dv.n
    p.w("tensor", s_dv, m)
    for c in range(NFI):
        p.op("tensor", "matmul", qps, wqtf_b[:, c * 128:(c + 1) * 128],
             ysb[:, c * 4:(c + 1) * 4], start=(c == 0), stop=(c == NFI - 1),
             inc=s_pe if c == NFI - 1 else None)
    p.w("vector", s_pe, s_pe.n)
    p.op("vector", "tensor_copy", q4[:, :], qps, inc=s_dv)
    p.w("vector", s_dv, s_dv.n)
    p.op("vector", "tensor_reduce", q_sb[:, :], q4[:, :], AX.X, ALU.add, inc=s_dv)
    p.w("scalar", s_dv, s_dv.n)
    p.op("scalar", "dma_start", inc=s_in, inck=16,
         out=ccin.ap().rearrange("a k -> k a"), in_=q_sb[:, :])
    p.w("gpsimd", s_in, s_in.n)
    p.op("gpsimd", "collective_compute", "AllGather", ALU.bypass,
         replica_groups=[[0, 1], [2, 3], [4, 5], [6, 7]],
         ins=[ccin.ap()], outs=[ccout.ap()], inc=s_cc)
    p.w("sync", s_cc, s_cc.n)
    p.op("sync", "dma_start", inc=s_in, inck=16, out=qa[:, :], in_=ccout.ap()[0:1, :])
    p.op("sync", "dma_start", inc=s_in, inck=16, out=qb[:, :], in_=ccout.ap()[1:2, :])
    p.w("vector", s_in, s_in.n)
    p.op("vector", "tensor_add", qt[:, :], qa[:, :], qb[:, :])
    p.op("vector", "tensor_copy", qt_bf[:, :], qt[:, :], inc=s_dv)
    p.w("tensor", s_dv, s_dv.n)
    p.op("tensor", "matmul", qcps, qt_bf[:, :], ones_1[:, :],
         start=True, stop=True, inc=s_pe)
    p.w("vector", s_pe, s_pe.n)
    p.op("vector", "tensor_copy", qc_bf[:, :], qcps, inc=s_dv)
    p.w("tensor", s_dv, s_dv.n)
    p.op("tensor", "matmul", scps, keysT_b[:, :], qc_bf[:, :],
         start=True, stop=True, inc=s_pe)
    p.w("vector", s_pe, s_pe.n)
    p.op("vector", "tensor_copy", sc_bf[:, :], scps, inc=s_dv)
    p.w("tensor", s_dv, s_dv.n)
    p.op("tensor", "matmul", sctps, sc_bf[:, :], i8_b[:, :],
         start=True, stop=True, inc=s_pe)
    p.w("scalar", s_pe, s_pe.n)
    p.op("scalar", "activation", e8[:, :], sctps, AF.Exp,
         scale=float(1.0 / (T * np.sqrt(128.0))), accum_out=se[:, :], inc=s_ac)
    p.w("vector", s_ac, s_ac.n)
    p.op("vector", "reciprocal", rse[:, :], se[:, :], inc=s_dv)
    p.w("scalar", s_dv, s_dv.n)
    p.op("scalar", "activation", cf[:, :], e8[:, :], AF.Copy,
         scale=rse[0:1, 0:1], inc=s_ac)
    p.w("vector", s_ac, s_ac.n)
    p.op("vector", "tensor_copy", cf_bf[:, :], cf[:, :], inc=s_dv)
    p.w("tensor", s_dv, s_dv.n)
    p.op("tensor", "matmul", ctps, cf_bf[:, :], ones_1[:, :],
         start=True, stop=True, inc=s_pe)
    p.w("vector", s_pe, s_pe.n)
    p.op("vector", "tensor_copy", ct_bf[:, :], ctps, inc=s_dv)
    p.w("tensor", s_dv, s_dv.n)
    for ecx in range(4):
        p.op("tensor", "matmul", ps_a[1][:, ecx:ecx + 1],
             ind_b[:, ecx * 128:(ecx + 1) * 128],
             ct_bf[:, :], start=True, stop=True,
             inc=s_pe if ecx == 3 else None)
    p.w("vector", s_pe, s_pe.n)
    p.op("vector", "tensor_copy", c_bc[:, 0:4], ps_a[1][:, 0:4], inc=s_dv)



def _perm128():
    return np.concatenate([np.arange(0, 128, 2), np.arange(1, 128, 2)])


def _build_host_constants():
    pm = _perm128()
    j = np.arange(64)
    freqs = 1.0 / (10000.0 ** (2 * j / 128.0))
    ang = np.outer(freqs, np.arange(T))
    cosk = np.cos(ang).astype(np.float32)
    sink = np.sin(ang).astype(np.float32)
    i8 = np.eye(8, dtype=np.float32)
    ind = (np.arange(512)[None, :] // 64 == np.arange(8)[:, None]).astype(np.float32)
    cst = np.zeros((128, 8), np.float32)
    cst[:, 0] = EXP_SHIFT
    masks = {}
    cosq = {}
    sinq = {}
    for role in (0, 1):
        own = ROLE_CHUNKS[role]
        m = np.zeros((128, 4 * 2 * 4 * CH), np.float32)
        for s in range(4):
            cs = CCOUNT[s]
            cq = own[s]
            for which in range(2):
                ck = cs - 2 + which
                for ksub in range(4):
                    base = ((s * 2 + which) * 4 + ksub) * CH
                    kglob = ck * CH + ksub * 128 + np.arange(128)[:, None]
                    qglob = cq * CH + np.arange(CH)[None, :]
                    m[:, base:base + CH] = (kglob <= qglob)
        masks[role] = m
        own_cols = np.concatenate([np.arange(cq * CH, (cq + 1) * CH) for cq in own])
        cosq[role] = np.ascontiguousarray(cosk[:, own_cols])
        sinq[role] = np.ascontiguousarray(sink[:, own_cols])
    return pm, cosk, sink, i8, ind, cst, masks, cosq, sinq


def kernel(**inputs):
    x = np.asarray(inputs["L0_attn_x"], dtype=np.float32)
    Wq = np.asarray(inputs["L0_attn_Wq"], dtype=np.float32)
    Wk = np.asarray(inputs["L0_attn_Wk"], dtype=np.float32)
    Wv = np.asarray(inputs["L0_attn_Wv"], dtype=np.float32)
    Wo = np.asarray(inputs["L0_attn_Wo"], dtype=np.float32)
    bo = np.asarray(inputs["L0_attn_bo"], dtype=np.float32)
    Worg = np.asarray(inputs["L0_ffn_W_org"], dtype=np.float32)
    ld = np.asarray(inputs["L0_ffn_lora_down"], dtype=np.float32)
    lu = np.asarray(inputs["L0_ffn_lora_up"], dtype=np.float32)
    lk = np.asarray(inputs["L0_ffn_lora_keys"], dtype=np.float32)
    fWq = np.asarray(inputs["L0_ffn_Wq"], dtype=np.float32)

    pm, cosk, sink, i8, ind, cst, masks, cosq, sinq = _build_host_constants()
    perm_q = np.concatenate([h * 128 + pm for h in range(H)])
    perm_k = np.concatenate([g * 128 + pm for g in range(G)])
    wq_p = np.ascontiguousarray(Wq[:, perm_q])
    wk_p = np.ascontiguousarray(Wk[:, perm_k])
    wv_c = np.ascontiguousarray(Wv)
    wo_c = np.ascontiguousarray(Wo)
    worgT = np.ascontiguousarray(Worg.T)
    down_all = np.ascontiguousarray(ld.transpose(1, 0, 2).reshape(C, 512))
    up_all = np.ascontiguousarray(lu.reshape(512, C))
    wqtf = np.ascontiguousarray(fWq.T)
    keysT = np.ascontiguousarray(lk.T)
    bo_t = np.ascontiguousarray(bo.reshape(NFI, 128).T)

    xTb = [np.ascontiguousarray(x[b].T) for b in range(B)]
    in_maps = []
    for core in range(8):
        b, role = core // 2, core % 2
        own = ROLE_CHUNKS[role]
        own_cols = np.concatenate([np.arange(cq * CH, (cq + 1) * CH) for cq in own])
        in_maps.append({
            "xT_own": np.ascontiguousarray(xTb[b][:, own_cols]),
            "xT_full": xTb[b],
            "wq": wq_p, "wk": wk_p, "wv": wv_c, "wo": wo_c, "bo_t": bo_t,
            "worgT": worgT, "down": down_all, "up": up_all, "wqtf": wqtf,
            "keysT": keysT, "i8": i8, "ind": ind,
            "cosk": cosk, "sink": sink,
            "cosq": cosq[role], "sinq": sinq[role],
            "maskf": masks[role], "cst": cst,
        })

    nc = build_program()
    res = run_bass_kernel_spmd(nc, in_maps, core_ids=list(range(8)))

    out = np.empty((B, T, C), np.float32)
    for core in range(8):
        b, role = core // 2, core % 2
        rT = np.asarray(res.results[core]["OUT"]).T
        for s, cq in enumerate(ROLE_CHUNKS[role]):
            out[b, cq * CH:(cq + 1) * CH, :] = rT[s * CH:(s + 1) * CH, :]
    if DBG["stage"] is not None:
        kernel._dbg = res
    return out


# version 02
# speedup vs baseline: 1.3099x; 1.3099x over previous
import numpy as np
import ml_dtypes
from contextlib import ExitStack

import concourse.bass as bass
import concourse.mybir as mybir
from concourse.bass_utils import run_bass_kernel_spmd

F32 = mybir.dt.float32
BF16 = mybir.dt.bfloat16
AF = mybir.ActivationFunctionType
ALU = mybir.AluOpType
AX = mybir.AxisListType

B, T, C = 4, 4096, 4096
H, G, D = 32, 8, 128
CH = 512
TOK = 2048
NFI = C // 128
ROLE_CHUNKS = {0: [0, 3, 4, 7], 1: [1, 2, 5, 6]}
CCOUNT = [2, 4, 6, 8]
RSQD = 1.0 / np.sqrt(128.0)
EXP_SHIFT = -8.0

DBG = {"stage": None}

ENGS = ("sync", "tensor", "scalar", "vector", "gpsimd")


class Sem:
    def __init__(self, h):
        self.h = h
        self.n = 0


class Plan:

    def __init__(self, block):
        self.block = block
        self.ops = {e: [] for e in ENGS}

    def do(self, eng, fn):
        self.ops[eng].append(fn)

    def op(self, eng, meth, *a, inc=None, inck=1, **kw):
        if inc is not None:
            inc.n += inck
            s, k = inc.h, inck
            self.ops[eng].append(
                lambda e, m=meth, a=a, kw=kw, s=s, k=k: getattr(e, m)(*a, **kw).then_inc(s, k))
        else:
            self.ops[eng].append(
                lambda e, m=meth, a=a, kw=kw: getattr(e, m)(*a, **kw))

    def w(self, eng, sem, val):
        if val > 0:
            self.ops[eng].append(lambda e, s=sem.h, v=val: e.wait_ge(s, v))

    def flush(self):
        for eng in ENGS:
            fns = self.ops[eng]
            if fns:
                getattr(self.block, eng)(lambda e, fs=tuple(fns): [f(e) for f in fs])
            self.ops[eng] = []


def barrier(p, sems):
    sb = sems["bar"]
    for k in ("in", "ld0", "ld1", "ld2", "ld3"):
        p.w("sync", sems[k], sems[k].n)
    for k in ("gi", "gl0", "gl1"):
        p.w("gpsimd", sems[k], sems[k].n)
    for k in ("st", "sc0", "sc1", "sc2", "sc3"):
        p.w("scalar", sems[k], sems[k].n)
    for eng in ENGS:
        sb.n += 1
        p.do(eng, lambda e, s=sb.h: e.nop().then_inc(s, 1))
    tgt = sb.n
    for eng in ENGS:
        p.w(eng, sb, tgt)


def build_program():
    nc = bass.Bass("TRN2", target_bir_lowering=False)

    def inp(name, shp):
        return nc.dram_tensor(name, shp, F32, kind="ExternalInput").ap()

    xT_own = inp("xT_own", [C, TOK])
    xT_full = inp("xT_full", [C, T])
    wq = inp("wq", [C, C])
    wk = inp("wk", [C, G * D])
    wv = inp("wv", [C, G * D])
    wo = inp("wo", [C, C])
    bo_t = inp("bo_t", [128, NFI])
    worgT = inp("worgT", [C, C])
    down = inp("down", [C, 512])
    up = inp("up", [512, C])
    wqtf = inp("wqtf", [C, 128])
    keysT = inp("keysT", [128, 8])
    i8 = inp("i8", [8, 8])
    ind = inp("ind", [8, 512])
    cosk = inp("cosk", [64, T])
    sink = inp("sink", [64, T])
    cosq = inp("cosq", [64, TOK])
    sinq = inp("sinq", [64, TOK])
    maskf = inp("maskf", [128, 4 * 2 * 4 * CH])
    cst = inp("cst", [128, 8])
    OUT = nc.dram_tensor("OUT", [C, TOK], F32, kind="ExternalOutput").ap()

    ksc = nc.dram_tensor("ksc", [G * D, T], BF16, kind="Internal").ap()
    vsc = nc.dram_tensor("vsc", [T, G * D], BF16, kind="Internal").ap()
    qsc = nc.dram_tensor("qsc", [C, TOK], BF16, kind="Internal").ap()
    asc = nc.dram_tensor("asc", [C, TOK], BF16, kind="Internal").ap()
    rsc = nc.dram_tensor("rsc", [C, TOK], F32, kind="Internal").ap()
    ysc = nc.dram_tensor("ysc", [C, TOK], F32, kind="Internal").ap()
    ccin = nc.dram_tensor("ccin", [1, 128], F32, kind="Internal")
    ccout = nc.dram_tensor("ccout", [2, 128], F32, kind="Internal")

    dbg_outs = {}
    stage = DBG["stage"]
    if stage is not None:
        for nm, src, shp, dt in [
            ("dbg_k", ksc, [G * D, T], BF16), ("dbg_v", vsc, [T, G * D], BF16),
            ("dbg_q", qsc, [C, TOK], BF16), ("dbg_a", asc, [C, TOK], BF16),
            ("dbg_r", rsc, [C, TOK], F32), ("dbg_y", ysc, [C, TOK], F32),
        ]:
            dbg_outs[nm] = (nc.dram_tensor(nm, shp, dt, kind="ExternalOutput").ap(), src)

    with ExitStack() as top:
        ec = top.enter_context
        sems = {k: Sem(ec(nc.semaphore("s_" + k)))
                for k in ("bar", "in", "gi", "pe", "ac", "dv", "st", "cc",
                          "ld0", "ld1", "ld2", "ld3",
                          "gl0", "gl1", "sc0", "sc1", "sc2", "sc3")}
        s_in, s_gi = sems["in"], sems["gi"]
        s_pe, s_ac, s_dv, s_st = sems["pe"], sems["ac"], sems["dv"], sems["st"]
        LD = [sems["ld%d" % i] for i in range(4)]
        GL = [sems["gl%d" % i] for i in range(2)]
        SC = [sems["sc%d" % i] for i in range(4)]

        cst_b = ec(nc.sbuf_tensor("cst_b", [128, 8], F32))
        bo_b = ec(nc.sbuf_tensor("bo_b", [128, NFI], F32))
        ones_c = ec(nc.sbuf_tensor("ones_c", [128, 1], BF16))
        ones_r = ec(nc.sbuf_tensor("ones_r", [1, 128], BF16))
        ones_1 = ec(nc.sbuf_tensor("ones_1", [1, 1], BF16))
        i8_b = ec(nc.sbuf_tensor("i8_b", [8, 8], BF16))
        ind_b = ec(nc.sbuf_tensor("ind_b", [8, 512], BF16))
        keysT_b = ec(nc.sbuf_tensor("keysT_b", [128, 8], BF16))
        wqtf_b = ec(nc.sbuf_tensor("wqtf_b", [128, NFI * 128], BF16))
        ysum = ec(nc.sbuf_tensor("ysum", [128, NFI * 4], F32))
        ssum = ec(nc.sbuf_tensor("ssum", [1, TOK], F32))
        ssq = ec(nc.sbuf_tensor("ssq", [1, TOK], F32))
        c_bc = ec(nc.sbuf_tensor("c_bc", [128, 4], F32))
        block = ec(nc.Block())
        p = Plan(block)

        p.op("sync", "dma_start", inc=s_in, inck=16, out=cst_b[:, :], in_=cst)
        p.op("sync", "dma_start", inc=s_in, inck=16, out=bo_b[:, :], in_=bo_t)
        p.op("gpsimd", "dma_start", inc=s_gi, inck=16, out=i8_b[:, :], in_=i8)
        p.op("gpsimd", "dma_start", inc=s_gi, inck=16, out=ind_b[:, :], in_=ind)
        p.op("gpsimd", "dma_start", inc=s_gi, inck=16, out=keysT_b[:, :], in_=keysT)
        p.op("gpsimd", "dma_start", inc=s_gi, inck=16,
             out=wqtf_b[:, :].rearrange("p (c k) -> p c k", c=NFI),
             in_=wqtf.rearrange("(c p) k -> p c k", p=128))
        p.op("vector", "memset", ones_c[:, :], 1.0)
        p.op("vector", "memset", ones_r[:, :], 1.0)
        p.op("vector", "memset", ones_1[:, :], 1.0)
        p.op("vector", "memset", ysum[:, :], 0.0)
        p.op("vector", "memset", ssum[:, :], 0.0)
        p.op("vector", "memset", ssq[:, :], 0.0)
        barrier(p, sems)
        p.flush()

        with ExitStack() as stA:
            eA = stA.enter_context

            evb = [eA(nc.sbuf_tensor(f"evb{i}", [128, 512], BF16)) for i in range(4)]
            rt_a = eA(nc.sbuf_tensor("rt_a", [64, 512], F32))
            rt_b2 = eA(nc.sbuf_tensor("rt_b2", [64, 512], F32))
            ps = [eA(nc.psum_tensor(f"ps{i}", [128, 512], F32)) for i in range(2)]
            stVK = ExitStack()
            eVK = stVK.enter_context
            w_big = eVK(nc.sbuf_tensor("w_big", [128, NFI * 1024], BF16))
            xb = [eVK(nc.sbuf_tensor(f"xb{i}", [128, NFI * CH], BF16)) for i in range(2)]
            cos_b = eVK(nc.sbuf_tensor("cos_b", [64, T], F32))
            sin_b = eVK(nc.sbuf_tensor("sin_b", [64, T], F32))

            p.op("sync", "dma_start", inc=s_in, inck=16, out=cos_b[:, :], in_=cosk)
            p.op("sync", "dma_start", inc=s_in, inck=16, out=sin_b[:, :], in_=sink)

            def stream_proj(kind):
                w_in = wv if kind == "v" else wk
                p.op("gpsimd", "dma_start", inc=s_gi, inck=16,
                     out=w_big[:, :].rearrange("p (c v) -> p c v", c=NFI),
                     in_=w_in.rearrange("(c p) v -> p c v", p=128))
                w_mark = s_gi.n
                x_marks = {}
                grp_done = {}
                ev_free = {}
                ps_free = {}
                tile_i = 0
                for tg in range(8):
                    if tg >= 2:
                        p.w("gpsimd", s_pe, grp_done[(tg - 2, "all")])
                    p.op("gpsimd", "dma_start", inc=GL[tg % 2], inck=16,
                         out=xb[tg % 2][:, :].rearrange("p (c t) -> p c t", c=NFI),
                         in_=xT_full[:, tg * CH:(tg + 1) * CH]
                         .rearrange("(c p) t -> p c t", p=128))
                    x_marks[tg] = GL[tg % 2].n
                    p.w("tensor", s_gi, w_mark)
                    p.w("tensor", GL[tg % 2], x_marks[tg])
                    ntile = 8
                    for j in range(ntile):
                        rot = tile_i % 2
                        if tile_i >= 2:
                            p.w("tensor", s_ac if kind == "v" else s_dv, ps_free[rot])
                        if kind == "v":
                            tsub, vh = j // 2, j % 2
                            for c in range(NFI):
                                p.op("tensor", "matmul", ps[rot][:, :],
                                     xb[tg % 2][:, c * CH + tsub * 128: c * CH + (tsub + 1) * 128],
                                     w_big[:, c * 1024 + vh * 512: c * 1024 + (vh + 1) * 512],
                                     start=(c == 0), stop=(c == NFI - 1),
                                     inc=s_pe if c == NFI - 1 else None)
                        else:
                            g = j
                            for c in range(NFI):
                                p.op("tensor", "matmul", ps[rot][:, :],
                                     w_big[:, c * 1024 + g * 128: c * 1024 + (g + 1) * 128],
                                     xb[tg % 2][:, c * CH:(c + 1) * CH],
                                     start=(c == 0), stop=(c == NFI - 1),
                                     inc=s_pe if c == NFI - 1 else None)
                        mm_mark = s_pe.n
                        erot = tile_i % 4
                        if tile_i >= 4:
                            eng = "scalar" if kind == "v" else "vector"
                            p.w(eng, SC[erot], ev_free[erot])
                        if kind == "v":
                            p.w("scalar", s_pe, mm_mark)
                            p.op("scalar", "copy", evb[erot][:, :], ps[rot][:, :], inc=s_ac)
                            ps_free[rot] = s_ac.n
                            p.w("scalar", s_ac, s_ac.n)
                            p.op("scalar", "dma_start", inc=SC[erot], inck=16,
                                 out=vsc[tg * CH + tsub * 128: tg * CH + (tsub + 1) * 128,
                                         vh * 512:(vh + 1) * 512],
                                 in_=evb[erot][:, :])
                        else:
                            p.w("vector", s_pe, mm_mark)
                            cs_ = cos_b[:, tg * CH:(tg + 1) * CH]
                            sn_ = sin_b[:, tg * CH:(tg + 1) * CH]
                            pr, pi = ps[rot][0:64, :], ps[rot][64:128, :]
                            p.op("vector", "tensor_mul", rt_a[:, :], pr, cs_)
                            p.op("vector", "tensor_mul", rt_b2[:, :], pi, sn_)
                            p.op("vector", "tensor_sub", evb[erot][0:64, :], rt_a[:, :], rt_b2[:, :])
                            p.op("vector", "tensor_mul", rt_a[:, :], pr, sn_)
                            p.op("vector", "tensor_mul", rt_b2[:, :], pi, cs_)
                            p.op("vector", "tensor_add", evb[erot][64:128, :],
                                 rt_a[:, :], rt_b2[:, :], inc=s_dv)
                            ps_free[rot] = s_dv.n
                            p.w("scalar", s_dv, s_dv.n)
                            p.op("scalar", "dma_start", inc=SC[erot], inck=16,
                                 out=ksc[g * 128:(g + 1) * 128, tg * CH:(tg + 1) * CH],
                                 in_=evb[erot][:, :])
                        ev_free[erot] = SC[erot].n
                        tile_i += 1
                    grp_done[(tg, "all")] = s_pe.n

            p.w("tensor", s_in, s_in.n)
            stream_proj("v")
            barrier(p, sems)
            stream_proj("k")
            barrier(p, sems)
            p.flush()
            stVK.close()

            with ExitStack() as stQ:
                eQ = stQ.enter_context
                xob = eQ(nc.sbuf_tensor("xob", [128, NFI * TOK], BF16))
                wqb = [eQ(nc.sbuf_tensor(f"wqb{i}", [128, NFI * 128], BF16)) for i in range(2)]
                cq_b = eQ(nc.sbuf_tensor("cq_b", [64, TOK], F32))
                sq_b = eQ(nc.sbuf_tensor("sq_b", [64, TOK], F32))

                p.op("sync", "dma_start", inc=s_in, inck=16, out=cq_b[:, :], in_=cosq)
                p.op("sync", "dma_start", inc=s_in, inck=16, out=sq_b[:, :], in_=sinq)
                for to in range(4):
                    p.op("gpsimd", "dma_start", inc=s_gi, inck=16,
                         out=xob[:, :].rearrange("p (c t) -> p c t", c=NFI)[:, :, to * CH:(to + 1) * CH],
                         in_=xT_own[:, to * CH:(to + 1) * CH].rearrange("(c p) t -> p c t", p=128))
                xo_mark = s_gi.n
                p.w("tensor", s_gi, xo_mark)
                p.w("tensor", s_in, s_in.n)
                wq_marks = {}
                hd_done = {}
                ev_free = {}
                ps_free = {}
                tile_i = 0
                for hd in range(H):
                    if hd >= 2:
                        p.w("gpsimd", s_pe, hd_done[hd - 2])
                    p.op("gpsimd", "dma_start", inc=GL[hd % 2], inck=16,
                         out=wqb[hd % 2][:, :].rearrange("p (c k) -> p c k", c=NFI),
                         in_=wq[:, hd * 128:(hd + 1) * 128].rearrange("(c p) k -> p c k", p=128))
                    wq_marks[hd] = GL[hd % 2].n
                    p.w("tensor", GL[hd % 2], wq_marks[hd])
                    for to in range(4):
                        rot = tile_i % 2
                        if tile_i >= 2:
                            p.w("tensor", s_dv, ps_free[rot])
                        for c in range(NFI):
                            p.op("tensor", "matmul", ps[rot][:, :],
                                 wqb[hd % 2][:, c * 128:(c + 1) * 128],
                                 xob[:, c * TOK + to * CH: c * TOK + (to + 1) * CH],
                                 start=(c == 0), stop=(c == NFI - 1),
                                 inc=s_pe if c == NFI - 1 else None)
                        mm_mark = s_pe.n
                        erot = tile_i % 4
                        if tile_i >= 4:
                            p.w("vector", SC[erot], ev_free[erot])
                        p.w("vector", s_pe, mm_mark)
                        cs_ = cq_b[:, to * CH:(to + 1) * CH]
                        sn_ = sq_b[:, to * CH:(to + 1) * CH]
                        pr, pi = ps[rot][0:64, :], ps[rot][64:128, :]
                        p.op("vector", "tensor_mul", rt_a[:, :], pr, cs_)
                        p.op("vector", "tensor_mul", rt_b2[:, :], pi, sn_)
                        p.op("vector", "tensor_sub", evb[erot][0:64, :], rt_a[:, :], rt_b2[:, :])
                        p.op("vector", "tensor_mul", rt_a[:, :], pr, sn_)
                        p.op("vector", "tensor_mul", rt_b2[:, :], pi, cs_)
                        p.op("vector", "tensor_add", evb[erot][64:128, :],
                             rt_a[:, :], rt_b2[:, :], inc=s_dv)
                        ps_free[rot] = s_dv.n
                        p.w("scalar", s_dv, s_dv.n)
                        p.op("scalar", "dma_start", inc=SC[erot], inck=16,
                             out=qsc[hd * 128:(hd + 1) * 128, to * CH:(to + 1) * CH],
                             in_=evb[erot][:, :])
                        ev_free[erot] = SC[erot].n
                        tile_i += 1
                    hd_done[hd] = s_pe.n
                barrier(p, sems)
                p.flush()

        if stage in ("kv", "q"):
            emit_debug(p, sems, dbg_outs, stage)
            p.flush()
            return nc

        with ExitStack() as st4:
            e4 = st4.enter_context
            mask_b = e4(nc.sbuf_tensor("mask_b", [128, 4 * 2 * 4 * CH], BF16))
            kgb = [e4(nc.sbuf_tensor(f"kgb{i}", [128, T], BF16)) for i in range(2)]
            vgb = [e4(nc.sbuf_tensor(f"vgb{i}", [128, 32 * 130], BF16)) for i in range(2)]
            qhb = [e4(nc.sbuf_tensor(f"qhb{i}", [128, TOK], BF16)) for i in range(2)]
            p_sb = [e4(nc.sbuf_tensor(f"p_sb{i}", [128, 1024], BF16)) for i in range(3)]
            rsum = e4(nc.sbuf_tensor("rsum", [1, 512], F32))
            rsum0 = e4(nc.sbuf_tensor("rsum0", [1, 512], F32))
            rhi = e4(nc.sbuf_tensor("rhi", [1, 512], BF16))
            rhf = e4(nc.sbuf_tensor("rhf", [1, 512], F32))
            rlf = e4(nc.sbuf_tensor("rlf", [1, 512], F32))
            rlo = e4(nc.sbuf_tensor("rlo", [1, 512], BF16))
            rt_sb = [e4(nc.sbuf_tensor(f"rt_sb{i}", [128, 512], F32)) for i in range(2)]
            at_sb = [e4(nc.sbuf_tensor(f"at_sb{i}", [128, 512], BF16)) for i in range(2)]
            st_ps = e4(nc.psum_tensor("st_ps", [128, 2048], F32))
            pva_ps = e4(nc.psum_tensor("pva_ps", [128, 512], F32))
            pvb_ps = e4(nc.psum_tensor("pvb_ps", [128, 512], F32))
            rt_ps = e4(nc.psum_tensor("rt_ps", [128, 512], F32))

            p.op("gpsimd", "dma_start", inc=s_gi, inck=16, out=mask_b[:, :], in_=maskf)
            mask_mark = s_gi.n
            for i in range(2):
                vb0 = vgb[i][:, :].rearrange("p (ks c) -> p ks c", c=130)
                p.op("vector", "memset", vb0[:, :, 64:65], 1.0)
                p.op("vector", "memset", vb0[:, :, 129:130], 1.0, inc=s_dv if i == 1 else None)
            ones_cols_mark = s_dv.n

            def load_kv(g, pe_wait):
                sem = LD[g % 2]
                p.w("sync", s_pe, pe_wait)
                p.op("sync", "dma_start", inc=sem, inck=16,
                     out=kgb[g % 2][:, :], in_=ksc[g * 128:(g + 1) * 128, :])
                vv = vsc[:, g * 128:(g + 1) * 128].rearrange("(ks p) d -> p ks d", p=128)
                vb = vgb[g % 2][:, :].rearrange("p (ks c) -> p ks c", c=130)
                for lo, hi in ((0, 16), (16, 32)):
                    p.op("sync", "dma_start", inc=sem, inck=16,
                         out=vb[:, lo:hi, 0:64], in_=vv[:, lo:hi, 0:64])
                    p.op("sync", "dma_start", inc=sem, inck=16,
                         out=vb[:, lo:hi, 65:129], in_=vv[:, lo:hi, 64:128])
                return (sem, sem.n)

            def load_q(h, pe_wait):
                sem = LD[2 + h % 2]
                p.w("sync", s_pe, pe_wait)
                p.op("sync", "dma_start", inc=sem, inck=16,
                     out=qhb[h % 2][:, :], in_=qsc[h * 128:(h + 1) * 128, :])
                return (sem, sem.n)

            kv_mark = {}
            q_mark = {}
            g_done = {-1: 0, -2: 0}
            h_done = {-1: 0, -2: 0}
            kv_mark[0] = load_kv(0, 0)
            q_mark[0] = load_q(0, 0)
            exp_mark = {}
            pair_consumed = {}
            at_st_mark = {0: 0, 1: 0}
            mask_dv_mark = {}

            slot_i = 0
            pending_tail = [None]

            def make_tail(s, h, g, pv_done, sp):
                def tail():
                    p.w("vector", s_pe, pv_done)
                    p.op("vector", "reciprocal", rsum[:, :], pva_ps[64:65, :], inc=s_dv)
                    p.w("vector", s_dv, s_dv.n)
                    p.op("vector", "tensor_copy", rhi[:, :], rsum[:, :])
                    p.op("vector", "tensor_copy", rhf[:, :], rhi[:, :])
                    p.op("vector", "tensor_sub", rlf[:, :], rsum[:, :], rhf[:, :])
                    p.op("vector", "tensor_copy", rlo[:, :], rlf[:, :], inc=s_dv)
                    hl_mark = s_dv.n
                    p.w("tensor", s_dv, hl_mark)
                    p.w("tensor", s_ac, tail_state["rt_cp"])
                    p.op("tensor", "matmul", rt_ps[:, :], ones_r[:, :], rhi[:, :],
                         start=True, stop=False)
                    p.op("tensor", "matmul", rt_ps[:, :], ones_r[:, :], rlo[:, :],
                         start=False, stop=True, inc=s_pe)
                    rt_mm = s_pe.n
                    p.w("scalar", s_pe, rt_mm)
                    p.op("scalar", "copy", rt_sb[sp][:, :], rt_ps[:, :], inc=s_ac)
                    tail_state["rt_cp"] = s_ac.n
                    p.w("vector", s_ac, tail_state["rt_cp"])
                    p.w("vector", SC[sp], at_st_mark[sp])
                    p.op("vector", "tensor_mul", at_sb[sp][0:64, :],
                         pva_ps[0:64, :], rt_sb[sp][0:64, :])
                    p.op("vector", "tensor_mul", at_sb[sp][64:128, :],
                         pvb_ps[0:64, :], rt_sb[sp][64:128, :], inc=s_dv)
                    tail_state["at_ev"] = s_dv.n
                    p.w("scalar", s_dv, tail_state["at_ev"])
                    p.op("scalar", "dma_start", inc=SC[sp], inck=16,
                         out=asc[h * 128:(h + 1) * 128, s * CH:(s + 1) * CH],
                         in_=at_sb[sp][:, :])
                    at_st_mark[sp] = SC[sp].n
                return tail

            tail_state = {"rt_cp": 0, "at_ev": 0}

            for g in range(G):
                if g + 1 < G:
                    kv_mark[g + 1] = load_kv(g + 1, g_done[g - 1])
                for hh in range(4):
                    h = g * 4 + hh
                    if h + 1 < H:
                        hq = h + 1
                        q_mark[hq] = load_q(hq, h_done[h - 1])
                    p.w("tensor", kv_mark[g][0], kv_mark[g][1])
                    p.w("tensor", q_mark[h][0], q_mark[h][1])
                    if h == 0:
                        p.w("vector", s_gi, mask_mark)
                        p.w("tensor", s_dv, ones_cols_mark)
                    for s in range(4):
                        cs = CCOUNT[s]
                        npair = 2 * cs
                        qs = qhb[h % 2][:, s * CH:(s + 1) * CH]
                        vsl = vgb[g % 2]

                        def emit_consume(pair, last):
                            kk0 = 2 * pair
                            ck = kk0 // 4
                            masked = ck >= cs - 2
                            p.w("tensor", s_ac, exp_mark[(slot_i, pair)])
                            if masked:
                                p.w("tensor", s_dv, mask_dv_mark[(slot_i, pair)])
                            if pair == 0:
                                p.w("tensor", s_dv, tail_state["at_ev"])
                            for dkk in range(2):
                                kk = kk0 + dkk
                                pt = p_sb[pair % 3][:, dkk * 512:(dkk + 1) * 512]
                                lst = last and dkk == 1
                                p.op("tensor", "matmul", pva_ps[0:65, :],
                                     vsl[:, kk * 130: kk * 130 + 65], pt,
                                     start=(pair == 0 and dkk == 0), stop=lst)
                                p.op("tensor", "matmul", pvb_ps[0:65, :],
                                     vsl[:, kk * 130 + 65: kk * 130 + 130], pt,
                                     start=(pair == 0 and dkk == 0), stop=lst,
                                     inc=s_pe if lst else None)
                            pair_consumed[(slot_i, pair)] = s_pe.n

                        for pair in range(npair):
                            kk0 = 2 * pair
                            half = pair % 2
                            if pair >= 2:
                                p.w("tensor", s_ac, exp_mark[(slot_i, pair - 2)])
                            for dkk in range(2):
                                kk = kk0 + dkk
                                p.op("tensor", "matmul",
                                     st_ps[:, half * 1024 + dkk * 512: half * 1024 + (dkk + 1) * 512],
                                     kgb[g % 2][:, kk * 128:(kk + 1) * 128], qs,
                                     start=True, stop=True,
                                     inc=s_pe if dkk == 1 else None)
                            st_mark = s_pe.n
                            p.w("scalar", s_pe, st_mark)
                            if pair >= 3:
                                p.w("scalar", s_pe, pair_consumed[(slot_i, pair - 3)])
                            p.op("scalar", "activation",
                                 p_sb[pair % 3][:, :], st_ps[:, half * 1024: half * 1024 + 1024],
                                 AF.Exp, bias=cst_b[:, 0:1], scale=RSQD, inc=s_ac)
                            exp_mark[(slot_i, pair)] = s_ac.n
                            ck = kk0 // 4
                            masked = ck >= cs - 2
                            if masked:
                                which = ck - (cs - 2)
                                base = ((s * 2 + which) * 4 + (kk0 % 4)) * CH
                                p.w("vector", s_ac, exp_mark[(slot_i, pair)])
                                p.op("vector", "tensor_mul", p_sb[pair % 3][:, :],
                                     p_sb[pair % 3][:, :], mask_b[:, base: base + 1024],
                                     inc=s_dv)
                                mask_dv_mark[(slot_i, pair)] = s_dv.n
                            if pair == 1 and pending_tail[0] is not None:
                                pending_tail[0]()
                                pending_tail[0] = None
                            if pair >= 1:
                                emit_consume(pair - 1, last=False)
                        emit_consume(npair - 1, last=True)
                        pv_done = s_pe.n
                        pending_tail[0] = make_tail(s, h, g, pv_done, s % 2)
                        slot_i += 1
                    h_done[h] = s_pe.n
                g_done[g] = s_pe.n
            if pending_tail[0] is not None:
                pending_tail[0]()
                pending_tail[0] = None
            barrier(p, sems)
            p.flush()

        if stage == "attn":
            emit_debug(p, sems, dbg_outs, stage)
            p.flush()
            return nc

        with ExitStack() as st5:
            e5 = st5.enter_context
            attn_b = e5(nc.sbuf_tensor("attn_b", [128, NFI * TOK], BF16))
            wob = [e5(nc.sbuf_tensor(f"wob{i}", [128, NFI * 128], BF16)) for i in range(2)]
            xo_f = [e5(nc.sbuf_tensor(f"xo_f{i}", [128, 512], F32)) for i in range(3)]
            r_f = [e5(nc.sbuf_tensor(f"r_f{i}", [128, 512], F32)) for i in range(3)]
            r_bb = [e5(nc.sbuf_tensor(f"r_bb{i}", [128, 512], BF16)) for i in range(2)]
            r2_bb = [e5(nc.sbuf_tensor(f"r2_bb{i}", [128, 512], BF16)) for i in range(2)]
            ps_wo = [e5(nc.psum_tensor(f"ps_wo{i}", [128, 512], F32)) for i in range(2)]
            st_s = e5(nc.psum_tensor("st_s", [128, 512], F32))
            st_q = e5(nc.psum_tensor("st_q", [128, 512], F32))

            p.op("sync", "dma_start", inc=s_in, inck=16,
                 out=attn_b[:, :].rearrange("p (hh q) -> p hh q", hh=NFI),
                 in_=asc.rearrange("(hh p) q -> p hh q", p=128))
            attn_mark = s_in.n
            emit_proj_res_stats(
                p, sems, nc, src_res=xT_own, res_scale_bias=True,
                w_dram=wo, w_bufs=wob, act_b=attn_b, xo_f=xo_f, r_f=r_f,
                r_bb=r_bb, r2_bb=r2_bb, ps_rot=ps_wo, st_s=st_s, st_q=st_q,
                ssum=ssum, ssq=ssq, bo_b=bo_b, out_dram=rsc,
                pre_marks=(attn_mark, 0), ones_c=ones_c, extra=None)
            barrier(p, sems)
            p.flush()

        if stage == "wo":
            emit_debug(p, sems, dbg_outs, stage)
            p.flush()
            return nc

        with ExitStack() as st6:
            e6 = st6.enter_context
            ps_a = [e6(nc.psum_tensor(f"ps_a{i}", [128, 512], F32)) for i in range(2)]
            stYH = ExitStack()
            st6.enter_context(stYH)
            y_b = stYH.enter_context(nc.sbuf_tensor("y_b", [128, NFI * TOK], BF16))

            with ExitStack() as st6i:
                e6i = st6i.enter_context
                mh = {}
                for nm in ("mh", "ml", "rh", "rl"):
                    mh[nm] = e6i(nc.sbuf_tensor("ln_" + nm, [1, TOK], BF16))
                tmp1 = e6i(nc.sbuf_tensor("ln_t1", [1, TOK], F32))
                tmp2 = e6i(nc.sbuf_tensor("ln_t2", [1, TOK], F32))
                M_sb = [e6i(nc.sbuf_tensor(f"M_sb{i}", [128, 512], F32)) for i in range(2)]
                R_sb = [e6i(nc.sbuf_tensor(f"R_sb{i}", [128, 512], F32)) for i in range(2)]
                rr_f = [e6i(nc.sbuf_tensor(f"rr_f{i}", [128, 512], F32)) for i in range(3)]
                yy_f = [e6i(nc.sbuf_tensor(f"yy_f{i}", [128, 512], F32)) for i in range(3)]
                ysp = e6i(nc.sbuf_tensor("ysp", [128, 1], F32))
                br_ps = e6i(nc.psum_tensor("br_ps", [128, 1024], F32))

                emit_ln_finalize(p, sems, ssum, ssq, tmp1, tmp2, mh)
                emit_ln_apply(p, sems, rsc, mh, tmp1, M_sb, R_sb, rr_f, yy_f,
                              br_ps, ones_r, out_dram=ysc, y_b=y_b, ysp=ysp, ysum=ysum,
                              out_dram2=None, OUT=None)
                barrier(p, sems)
                p.flush()

            if stage == "ln1":
                ys_o = nc.dram_tensor("dbg_ysum", [128, NFI * 4], F32,
                                      kind="ExternalOutput").ap()
                p.op("scalar", "dma_start", inc=s_st, inck=16,
                     out=ys_o, in_=ysum[:, :])
                p.w("scalar", s_st, s_st.n)
                emit_debug(p, sems, dbg_outs, stage)
                p.flush()
                return nc

            h_b = stYH.enter_context(nc.sbuf_tensor("h_b", [128, 4 * TOK], BF16))
            with ExitStack() as st7:
                e7 = st7.enter_context
                dnb = [e7(nc.sbuf_tensor(f"dnb{i}", [128, NFI * 128], BF16)) for i in range(2)]
                ysb = e7(nc.sbuf_tensor("cc_ysb", [128, NFI * 4], BF16))
                q4 = e7(nc.sbuf_tensor("cc_q4", [128, 4], F32))
                q_sb = e7(nc.sbuf_tensor("cc_qsb", [128, 1], F32))
                qa = e7(nc.sbuf_tensor("cc_qa", [1, 128], F32))
                qb = e7(nc.sbuf_tensor("cc_qb", [1, 128], F32))
                qt = e7(nc.sbuf_tensor("cc_qt", [1, 128], F32))
                qt_bf = e7(nc.sbuf_tensor("cc_qtb", [1, 128], BF16))
                qc_bf = e7(nc.sbuf_tensor("cc_qcb", [128, 1], BF16))
                sc_bf = e7(nc.sbuf_tensor("cc_scb", [8, 1], BF16))
                e8s = e7(nc.sbuf_tensor("cc_e8", [1, 8], F32))
                se = e7(nc.sbuf_tensor("cc_se", [1, 1], F32))
                rse = e7(nc.sbuf_tensor("cc_rse", [1, 1], F32))
                cf = e7(nc.sbuf_tensor("cc_cf", [1, 8], F32))
                cf_bf = e7(nc.sbuf_tensor("cc_cfb", [1, 8], BF16))
                ct_bf = e7(nc.sbuf_tensor("cc_ctb", [8, 1], BF16))

                cps = e7(nc.psum_tensor("cps", [128, 512], F32))
                cc_tiles = (ysb, q4, q_sb, qa, qb, qt, qt_bf, qc_bf, sc_bf,
                            e8s, se, rse, cf, cf_bf, ct_bf, ccin, ccout)
                emit_coeffs_pre(p, sems, ysum, wqtf_b, cc_tiles, cps)
                dn_marks = {}
                ec_done = {}
                ps_free = {}
                tile_i = 0
                for ecx in range(4):
                    if ecx >= 2:
                        p.w("gpsimd", s_pe, ec_done[ecx - 2])
                    p.op("gpsimd", "dma_start", inc=GL[ecx % 2], inck=16,
                         out=dnb[ecx % 2][:, :].rearrange("p (c k) -> p c k", c=NFI),
                         in_=down[:, ecx * 128:(ecx + 1) * 128].rearrange("(c p) k -> p c k", p=128))
                    dn_marks[ecx] = GL[ecx % 2].n
                    p.w("tensor", GL[ecx % 2], dn_marks[ecx])
                    for ts in range(4):
                        rot = tile_i % 2
                        if tile_i >= 2:
                            p.w("tensor", s_ac, ps_free[rot])
                        for c in range(NFI):
                            p.op("tensor", "matmul", ps_a[rot][:, :],
                                 dnb[ecx % 2][:, c * 128:(c + 1) * 128],
                                 y_b[:, c * TOK + ts * CH: c * TOK + (ts + 1) * CH],
                                 start=(c == 0), stop=(c == NFI - 1),
                                 inc=s_pe if c == NFI - 1 else None)
                        p.w("scalar", s_pe, s_pe.n)
                        p.op("scalar", "copy",
                             h_b[:, ecx * TOK + ts * CH: ecx * TOK + (ts + 1) * CH],
                             ps_a[rot][:, :], inc=s_ac)
                        ps_free[rot] = s_ac.n
                        tile_i += 1
                    ec_done[ecx] = s_pe.n
                emit_coeffs_post(p, sems, keysT_b, i8_b, ind_b, ones_1,
                                 cc_tiles, c_bc, cps)
                barrier(p, sems)
                p.flush()

            p.op("vector", "memset", ssum[:, :], 0.0)
            p.op("vector", "memset", ssq[:, :], 0.0)
            with ExitStack() as st8:
                e8 = st8.enter_context
                wgb = [e8(nc.sbuf_tensor(f"wgb{i}", [128, NFI * 128], BF16)) for i in range(2)]
                upb = [e8(nc.sbuf_tensor(f"upb{i}", [128, 4 * 128], BF16)) for i in range(2)]
                yo_f = [e8(nc.sbuf_tensor(f"yo_f{i}", [128, 512], F32)) for i in range(3)]
                r_f = [e8(nc.sbuf_tensor(f"r8_f{i}", [128, 512], F32)) for i in range(3)]
                r_bb = [e8(nc.sbuf_tensor(f"r8_bb{i}", [128, 512], BF16)) for i in range(2)]
                r2_bb = [e8(nc.sbuf_tensor(f"r8_2bb{i}", [128, 512], BF16)) for i in range(2)]
                st_s = e8(nc.psum_tensor("st8_s", [128, 512], F32))
                st_q = e8(nc.psum_tensor("st8_q", [128, 512], F32))

                emit_proj_res_stats(
                    p, sems, nc, src_res=ysc, res_scale_bias=False,
                    w_dram=worgT, w_bufs=wgb, act_b=y_b, xo_f=yo_f, r_f=r_f,
                    r_bb=r_bb, r2_bb=r2_bb, ps_rot=ps_a, st_s=st_s, st_q=st_q,
                    ssum=ssum, ssq=ssq, bo_b=None, out_dram=rsc,
                    pre_marks=(0, 0), ones_c=ones_c,
                    extra={"up": up, "upb": upb, "h_b": h_b, "c_bc": c_bc})
                barrier(p, sems)
                p.flush()
            stYH.close()

            with ExitStack() as st9:
                e9 = st9.enter_context
                mh = {}
                for nm in ("mh", "ml", "rh", "rl"):
                    mh[nm] = e9(nc.sbuf_tensor("l2_" + nm, [1, TOK], BF16))
                tmp1 = e9(nc.sbuf_tensor("l2_t1", [1, TOK], F32))
                tmp2 = e9(nc.sbuf_tensor("l2_t2", [1, TOK], F32))
                M_sb = [e9(nc.sbuf_tensor(f"M2_sb{i}", [128, 512], F32)) for i in range(2)]
                R_sb = [e9(nc.sbuf_tensor(f"R2_sb{i}", [128, 512], F32)) for i in range(2)]
                rr_f = [e9(nc.sbuf_tensor(f"r9_f{i}", [128, 512], F32)) for i in range(3)]
                yy_f = [e9(nc.sbuf_tensor(f"y9_f{i}", [128, 512], F32)) for i in range(3)]
                br_ps = e9(nc.psum_tensor("br2_ps", [128, 1024], F32))

                emit_ln_finalize(p, sems, ssum, ssq, tmp1, tmp2, mh)
                emit_ln_apply(p, sems, rsc, mh, tmp1, M_sb, R_sb, rr_f, yy_f,
                              br_ps, ones_r, out_dram=None, y_b=None, ysp=None,
                              ysum=None, out_dram2=None, OUT=OUT)
                for k in ("st", "sc0", "sc1", "sc2", "sc3"):
                    p.w("scalar", sems[k], sems[k].n)
                if stage == "ffn":
                    emit_debug(p, sems, dbg_outs, stage)
                p.flush()
    return nc


def emit_debug(p, sems, dbg_outs, stage):
    s_gi = sems["gi"]
    for nm, (out_ap, src) in dbg_outs.items():
        p.op("gpsimd", "dma_start", inc=s_gi, inck=16, out=out_ap, in_=src)
    p.w("gpsimd", s_gi, s_gi.n)


def emit_proj_res_stats(p, sems, nc, *, src_res, res_scale_bias, w_dram, w_bufs,
                        act_b, xo_f, r_f, r_bb, r2_bb, ps_rot, st_s, st_q,
                        ssum, ssq, bo_b, out_dram, pre_marks, ones_c, extra):
    s_in, s_gi = sems["in"], sems["gi"]
    s_pe, s_ac, s_dv, s_st = sems["pe"], sems["ac"], sems["dv"], sems["st"]
    LD = [sems["ld%d" % i] for i in range(4)]
    GL = [sems["gl%d" % i] for i in range(2)]
    SC = [sems["sc%d" % i] for i in range(4)]
    act_mark, _ = pre_marks
    w_marks = {}
    co_done = {}
    ps_free = {}
    xo_marks = {}
    r_free = {}
    stat_dv = 0
    tile_i = 0
    p.w("tensor", s_in, act_mark)
    for co in range(NFI):
        if co >= 2:
            p.w("gpsimd", s_pe, co_done[co - 2])
        p.op("gpsimd", "dma_start", inc=GL[co % 2], inck=16,
             out=w_bufs[co % 2][:, :].rearrange("p (c k) -> p c k", c=NFI),
             in_=w_dram[:, co * 128:(co + 1) * 128].rearrange("(c p) k -> p c k", p=128))
        if extra is not None:
            p.op("gpsimd", "dma_start", inc=GL[co % 2], inck=16,
                 out=extra["upb"][co % 2][:, :].rearrange("p (e k) -> p e k", e=4),
                 in_=extra["up"][:, co * 128:(co + 1) * 128].rearrange("(e p) k -> p e k", p=128))
        w_marks[co] = GL[co % 2].n
        p.w("tensor", GL[co % 2], w_marks[co])
        if extra is not None:
            p.w("vector", GL[co % 2], w_marks[co])
            if co >= 2:
                p.w("vector", s_pe, co_done[co - 2])
            for ecx in range(4):
                p.op("vector", "tensor_scalar",
                     extra["upb"][co % 2][:, ecx * 128:(ecx + 1) * 128],
                     extra["upb"][co % 2][:, ecx * 128:(ecx + 1) * 128],
                     extra["c_bc"][:, ecx:ecx + 1], None, ALU.mult,
                     inc=s_dv if ecx == 3 else None)
            up_scaled = s_dv.n
            p.w("tensor", s_dv, up_scaled)
        for ts in range(4):
            rot = tile_i % 2
            xrot = tile_i % 3
            if tile_i >= 3:
                p.w("sync", s_dv, r_free.get(("x", xrot), 0))
            p.op("sync", "dma_start", inc=LD[xrot], inck=16,
                 out=xo_f[xrot][:, :],
                 in_=src_res[co * 128:(co + 1) * 128, ts * CH:(ts + 1) * CH])
            xo_marks[tile_i] = LD[xrot].n
            if tile_i >= 2:
                p.w("tensor", s_dv, ps_free[rot])
            for c in range(NFI):
                p.op("tensor", "matmul", ps_rot[rot][:, :],
                     w_bufs[co % 2][:, c * 128:(c + 1) * 128],
                     act_b[:, c * TOK + ts * CH: c * TOK + (ts + 1) * CH],
                     start=(c == 0), stop=(extra is None and c == NFI - 1),
                     inc=(s_pe if (extra is None and c == NFI - 1) else None))
            if extra is not None:
                for ecx in range(4):
                    p.op("tensor", "matmul", ps_rot[rot][:, :],
                         extra["upb"][co % 2][:, ecx * 128:(ecx + 1) * 128],
                         extra["h_b"][:, ecx * TOK + ts * CH: ecx * TOK + (ts + 1) * CH],
                         start=False, stop=(ecx == 3),
                         inc=s_pe if ecx == 3 else None)
            mm_mark = s_pe.n
            p.w("vector", s_pe, mm_mark)
            p.w("vector", LD[xrot], xo_marks[tile_i])
            rrot = tile_i % 3
            if tile_i >= 3:
                p.w("vector", SC[rrot], r_free.get(("r", rrot), 0))
            if res_scale_bias:
                p.op("vector", "scalar_tensor_tensor", r_f[rrot][:, :],
                     ps_rot[rot][:, :], bo_b[:, co:co + 1], xo_f[xrot][:, :],
                     ALU.add, ALU.add)
            else:
                p.op("vector", "tensor_add", r_f[rrot][:, :],
                     ps_rot[rot][:, :], xo_f[xrot][:, :])
            brot = tile_i % 2
            p.op("vector", "tensor_copy", r_bb[brot][:, :], r_f[rrot][:, :], inc=s_dv)
            ps_free[rot] = s_dv.n
            r_free[("x", xrot)] = s_dv.n
            rb_mark = s_dv.n
            p.w("scalar", s_dv, rb_mark)
            p.op("scalar", "square", r2_bb[brot][:, :], r_bb[brot][:, :], inc=s_ac)
            sq_mark = s_ac.n
            p.w("tensor", s_dv, stat_dv)
            p.w("tensor", s_dv, rb_mark)
            p.w("tensor", s_ac, sq_mark)
            p.op("tensor", "matmul", st_s[0:1, :], ones_c[:, :], r_bb[brot][:, :],
                 start=True, stop=True)
            p.op("tensor", "matmul", st_q[0:1, :], ones_c[:, :], r2_bb[brot][:, :],
                 start=True, stop=True, inc=s_pe)
            stat_pe = s_pe.n
            p.w("vector", s_pe, stat_pe)
            p.op("vector", "tensor_add", ssum[0:1, ts * CH:(ts + 1) * CH],
                 ssum[0:1, ts * CH:(ts + 1) * CH], st_s[0:1, :])
            p.op("vector", "tensor_add", ssq[0:1, ts * CH:(ts + 1) * CH],
                 ssq[0:1, ts * CH:(ts + 1) * CH], st_q[0:1, :], inc=s_dv)
            stat_dv = s_dv.n
            p.w("scalar", s_dv, rb_mark)
            p.op("scalar", "dma_start", inc=SC[rrot], inck=16,
                 out=out_dram[co * 128:(co + 1) * 128, ts * CH:(ts + 1) * CH],
                 in_=r_f[rrot][:, :])
            r_free[("r", rrot)] = SC[rrot].n
            tile_i += 1
        co_done[co] = s_pe.n


def emit_ln_finalize(p, sems, ssum, ssq, tmp1, tmp2, mh):
    s_ac, s_dv = sems["ac"], sems["dv"]
    inv = 1.0 / C
    p.op("vector", "tensor_scalar", tmp1[:, :], ssum[:, :], inv, None, ALU.mult)
    p.op("vector", "tensor_copy", mh["mh"][:, :], tmp1[:, :])
    p.op("vector", "tensor_copy", tmp2[:, :], mh["mh"][:, :])
    p.op("vector", "tensor_sub", tmp2[:, :], tmp1[:, :], tmp2[:, :])
    p.op("vector", "tensor_copy", mh["ml"][:, :], tmp2[:, :])
    p.op("vector", "tensor_mul", tmp2[:, :], tmp1[:, :], tmp1[:, :])
    p.op("vector", "tensor_scalar", tmp1[:, :], ssq[:, :], inv, None, ALU.mult)
    p.op("vector", "tensor_sub", tmp1[:, :], tmp1[:, :], tmp2[:, :])
    p.op("vector", "tensor_scalar", tmp1[:, :], tmp1[:, :], 1e-5, None, ALU.add,
         inc=s_dv)
    vm = s_dv.n
    p.w("scalar", s_dv, vm)
    p.op("scalar", "sqrt", tmp2[:, :], tmp1[:, :], inc=s_ac)
    p.w("vector", s_ac, s_ac.n)
    p.op("vector", "reciprocal", tmp1[:, :], tmp2[:, :], inc=s_dv)
    p.w("vector", s_dv, s_dv.n)
    p.op("vector", "tensor_copy", mh["rh"][:, :], tmp1[:, :])
    p.op("vector", "tensor_copy", tmp2[:, :], mh["rh"][:, :])
    p.op("vector", "tensor_sub", tmp2[:, :], tmp1[:, :], tmp2[:, :])
    p.op("vector", "tensor_copy", mh["rl"][:, :], tmp2[:, :], inc=s_dv)


def emit_ln_apply(p, sems, r_dram, mh, scratch_row, M_sb, R_sb, rr_f, yy_f,
                  br_ps, ones_r, *, out_dram, y_b, ysp, ysum, out_dram2, OUT):
    s_in, s_pe = sems["in"], sems["pe"]
    s_ac, s_dv, s_st = sems["ac"], sems["dv"], sems["st"]
    LD = [sems["ld%d" % i] for i in range(4)]
    SC = [sems["sc%d" % i] for i in range(4)]
    hl_mark = s_dv.n
    r_marks = {}
    buf_free = {}
    mr_used = {}
    mr_mark = {}
    tile_i = 0
    for ts in range(4):
        p.w("tensor", s_dv, hl_mark)
        p.w("tensor", s_ac, mr_mark.get(ts - 1, 0))
        sl = slice(ts * CH, (ts + 1) * CH)
        p.op("tensor", "matmul", br_ps[:, 0:512], ones_r[:, :], mh["mh"][0:1, sl],
             start=True, stop=False)
        p.op("tensor", "matmul", br_ps[:, 0:512], ones_r[:, :], mh["ml"][0:1, sl],
             start=False, stop=True)
        p.op("tensor", "matmul", br_ps[:, 512:1024], ones_r[:, :], mh["rh"][0:1, sl],
             start=True, stop=False)
        p.op("tensor", "matmul", br_ps[:, 512:1024], ones_r[:, :], mh["rl"][0:1, sl],
             start=False, stop=True, inc=s_pe)
        br_mark = s_pe.n
        p.w("scalar", s_pe, br_mark)
        p.w("scalar", s_dv, mr_used.get(ts - 2, 0))
        p.op("scalar", "copy", M_sb[ts % 2][:, :], br_ps[:, 0:512])
        p.op("scalar", "copy", R_sb[ts % 2][:, :], br_ps[:, 512:1024], inc=s_ac)
        mr_mark[ts] = s_ac.n
        p.w("vector", s_ac, mr_mark[ts])
        for fi in range(NFI):
            rrot = tile_i % 3
            if tile_i >= 3:
                p.w("sync", s_dv, buf_free.get(("rr", rrot), 0))
            p.op("sync", "dma_start", inc=LD[rrot], inck=16,
                 out=rr_f[rrot][:, :], in_=r_dram[fi * 128:(fi + 1) * 128, sl])
            r_marks[tile_i] = LD[rrot].n
            p.w("vector", LD[rrot], r_marks[tile_i])
            yrot = tile_i % 3
            if tile_i >= 3:
                p.w("vector", SC[yrot], buf_free.get(("yy", yrot), 0))
            p.op("vector", "tensor_sub", yy_f[yrot][:, :], rr_f[rrot][:, :],
                 M_sb[ts % 2][:, :])
            p.op("vector", "tensor_mul", yy_f[yrot][:, :], yy_f[yrot][:, :],
                 R_sb[ts % 2][:, :], inc=s_dv)
            buf_free[("rr", rrot)] = s_dv.n
            y_mark = s_dv.n
            if y_b is not None:
                p.op("vector", "tensor_copy",
                     y_b[:, fi * TOK + ts * CH: fi * TOK + (ts + 1) * CH],
                     yy_f[yrot][:, :])
                p.op("vector", "tensor_reduce", ysum[:, fi * 4 + ts: fi * 4 + ts + 1],
                     yy_f[yrot][:, :], AX.X, ALU.add, inc=s_dv)
                y_mark = s_dv.n
            p.w("scalar", s_dv, y_mark)
            dst = OUT if OUT is not None else out_dram
            p.op("scalar", "dma_start", inc=SC[yrot], inck=16,
                 out=dst[fi * 128:(fi + 1) * 128, sl], in_=yy_f[yrot][:, :])
            buf_free[("yy", yrot)] = SC[yrot].n
            tile_i += 1
        mr_used[ts] = s_dv.n


def emit_coeffs_pre(p, sems, ysum, wqtf_b, tiles, cps):
    s_in = sems["in"]
    s_pe, s_dv, s_cc = sems["pe"], sems["dv"], sems["cc"]
    ysb, q4, q_sb = tiles[0], tiles[1], tiles[2]
    ccin, ccout = tiles[-2], tiles[-1]
    qps = cps[:, 0:4]
    p.op("vector", "tensor_copy", ysb[:, :], ysum[:, :], inc=s_dv)
    m = s_dv.n
    p.w("tensor", s_dv, m)
    for c in range(NFI):
        p.op("tensor", "matmul", qps, wqtf_b[:, c * 128:(c + 1) * 128],
             ysb[:, c * 4:(c + 1) * 4], start=(c == 0), stop=(c == NFI - 1),
             inc=s_pe if c == NFI - 1 else None)
    p.w("vector", s_pe, s_pe.n)
    p.op("vector", "tensor_copy", q4[:, :], qps, inc=s_dv)
    p.w("vector", s_dv, s_dv.n)
    p.op("vector", "tensor_reduce", q_sb[:, :], q4[:, :], AX.X, ALU.add, inc=s_dv)
    p.w("scalar", s_dv, s_dv.n)
    p.op("scalar", "dma_start", inc=s_in, inck=16,
         out=ccin.ap().rearrange("a k -> k a"), in_=q_sb[:, :])
    p.w("gpsimd", s_in, s_in.n)
    p.op("gpsimd", "collective_compute", "AllGather", ALU.bypass,
         replica_groups=[[0, 1], [2, 3], [4, 5], [6, 7]],
         ins=[ccin.ap()], outs=[ccout.ap()], inc=s_cc)


def emit_coeffs_post(p, sems, keysT_b, i8_b, ind_b, ones_1, tiles, c_bc, cps):
    s_in = sems["in"]
    s_pe, s_ac, s_dv, s_cc = sems["pe"], sems["ac"], sems["dv"], sems["cc"]
    (ysb, q4, q_sb, qa, qb, qt, qt_bf, qc_bf, sc_bf,
     e8, se, rse, cf, cf_bf, ct_bf, ccin, ccout) = tiles
    qcps = cps[:, 4:5]
    scps = cps[0:8, 8:9]
    sctps = cps[0:1, 12:20]
    ctps = cps[0:8, 24:25]
    cbps = cps[:, 28:32]
    p.w("sync", s_cc, sems["cc"].n)
    p.op("sync", "dma_start", inc=s_in, inck=16, out=qa[:, :], in_=ccout.ap()[0:1, :])
    p.op("sync", "dma_start", inc=s_in, inck=16, out=qb[:, :], in_=ccout.ap()[1:2, :])
    p.w("vector", s_in, s_in.n)
    p.op("vector", "tensor_add", qt[:, :], qa[:, :], qb[:, :])
    p.op("vector", "tensor_copy", qt_bf[:, :], qt[:, :], inc=s_dv)
    p.w("tensor", s_dv, s_dv.n)
    p.op("tensor", "matmul", qcps, qt_bf[:, :], ones_1[:, :],
         start=True, stop=True, inc=s_pe)
    p.w("vector", s_pe, s_pe.n)
    p.op("vector", "tensor_copy", qc_bf[:, :], qcps, inc=s_dv)
    p.w("tensor", s_dv, s_dv.n)
    p.op("tensor", "matmul", scps, keysT_b[:, :], qc_bf[:, :],
         start=True, stop=True, inc=s_pe)
    p.w("vector", s_pe, s_pe.n)
    p.op("vector", "tensor_copy", sc_bf[:, :], scps, inc=s_dv)
    p.w("tensor", s_dv, s_dv.n)
    p.op("tensor", "matmul", sctps, sc_bf[:, :], i8_b[:, :],
         start=True, stop=True, inc=s_pe)
    p.w("scalar", s_pe, s_pe.n)
    p.op("scalar", "activation", e8[:, :], sctps, AF.Exp,
         scale=float(1.0 / (T * np.sqrt(128.0))), accum_out=se[:, :], inc=s_ac)
    p.w("vector", s_ac, s_ac.n)
    p.op("vector", "reciprocal", rse[:, :], se[:, :], inc=s_dv)
    p.w("scalar", s_dv, s_dv.n)
    p.op("scalar", "activation", cf[:, :], e8[:, :], AF.Copy,
         scale=rse[0:1, 0:1], inc=s_ac)
    p.w("vector", s_ac, s_ac.n)
    p.op("vector", "tensor_copy", cf_bf[:, :], cf[:, :], inc=s_dv)
    p.w("tensor", s_dv, s_dv.n)
    p.op("tensor", "matmul", ctps, cf_bf[:, :], ones_1[:, :],
         start=True, stop=True, inc=s_pe)
    p.w("vector", s_pe, s_pe.n)
    p.op("vector", "tensor_copy", ct_bf[:, :], ctps, inc=s_dv)
    p.w("tensor", s_dv, s_dv.n)
    for ecx in range(4):
        p.op("tensor", "matmul", cbps[:, ecx:ecx + 1],
             ind_b[:, ecx * 128:(ecx + 1) * 128],
             ct_bf[:, :], start=True, stop=True,
             inc=s_pe if ecx == 3 else None)
    p.w("vector", s_pe, s_pe.n)
    p.op("vector", "tensor_copy", c_bc[:, 0:4], cbps, inc=s_dv)



def _perm128():
    return np.concatenate([np.arange(0, 128, 2), np.arange(1, 128, 2)])


def _build_host_constants():
    pm = _perm128()
    j = np.arange(64)
    freqs = 1.0 / (10000.0 ** (2 * j / 128.0))
    ang = np.outer(freqs, np.arange(T))
    cosk = np.cos(ang).astype(np.float32)
    sink = np.sin(ang).astype(np.float32)
    i8 = np.eye(8, dtype=np.float32)
    ind = (np.arange(512)[None, :] // 64 == np.arange(8)[:, None]).astype(np.float32)
    cst = np.zeros((128, 8), np.float32)
    cst[:, 0] = EXP_SHIFT
    masks = {}
    cosq = {}
    sinq = {}
    for role in (0, 1):
        own = ROLE_CHUNKS[role]
        m = np.zeros((128, 4 * 2 * 4 * CH), np.float32)
        for s in range(4):
            cs = CCOUNT[s]
            cq = own[s]
            for which in range(2):
                ck = cs - 2 + which
                for ksub in range(4):
                    base = ((s * 2 + which) * 4 + ksub) * CH
                    kglob = ck * CH + ksub * 128 + np.arange(128)[:, None]
                    qglob = cq * CH + np.arange(CH)[None, :]
                    m[:, base:base + CH] = (kglob <= qglob)
        masks[role] = m
        own_cols = np.concatenate([np.arange(cq * CH, (cq + 1) * CH) for cq in own])
        cosq[role] = np.ascontiguousarray(cosk[:, own_cols])
        sinq[role] = np.ascontiguousarray(sink[:, own_cols])
    return pm, cosk, sink, i8, ind, cst, masks, cosq, sinq


def kernel(**inputs):
    x = np.asarray(inputs["L0_attn_x"], dtype=np.float32)
    Wq = np.asarray(inputs["L0_attn_Wq"], dtype=np.float32)
    Wk = np.asarray(inputs["L0_attn_Wk"], dtype=np.float32)
    Wv = np.asarray(inputs["L0_attn_Wv"], dtype=np.float32)
    Wo = np.asarray(inputs["L0_attn_Wo"], dtype=np.float32)
    bo = np.asarray(inputs["L0_attn_bo"], dtype=np.float32)
    Worg = np.asarray(inputs["L0_ffn_W_org"], dtype=np.float32)
    ld = np.asarray(inputs["L0_ffn_lora_down"], dtype=np.float32)
    lu = np.asarray(inputs["L0_ffn_lora_up"], dtype=np.float32)
    lk = np.asarray(inputs["L0_ffn_lora_keys"], dtype=np.float32)
    fWq = np.asarray(inputs["L0_ffn_Wq"], dtype=np.float32)

    pm, cosk, sink, i8, ind, cst, masks, cosq, sinq = _build_host_constants()
    perm_q = np.concatenate([h * 128 + pm for h in range(H)])
    perm_k = np.concatenate([g * 128 + pm for g in range(G)])
    wq_p = np.ascontiguousarray(Wq[:, perm_q])
    wk_p = np.ascontiguousarray(Wk[:, perm_k])
    wv_c = np.ascontiguousarray(Wv)
    wo_c = np.ascontiguousarray(Wo)
    worgT = np.ascontiguousarray(Worg.T)
    down_all = np.ascontiguousarray(ld.transpose(1, 0, 2).reshape(C, 512))
    up_all = np.ascontiguousarray(lu.reshape(512, C))
    wqtf = np.ascontiguousarray(fWq.T)
    keysT = np.ascontiguousarray(lk.T)
    bo_t = np.ascontiguousarray(bo.reshape(NFI, 128).T)

    xTb = [np.ascontiguousarray(x[b].T) for b in range(B)]
    in_maps = []
    for core in range(8):
        b, role = core // 2, core % 2
        own = ROLE_CHUNKS[role]
        own_cols = np.concatenate([np.arange(cq * CH, (cq + 1) * CH) for cq in own])
        in_maps.append({
            "xT_own": np.ascontiguousarray(xTb[b][:, own_cols]),
            "xT_full": xTb[b],
            "wq": wq_p, "wk": wk_p, "wv": wv_c, "wo": wo_c, "bo_t": bo_t,
            "worgT": worgT, "down": down_all, "up": up_all, "wqtf": wqtf,
            "keysT": keysT, "i8": i8, "ind": ind,
            "cosk": cosk, "sink": sink,
            "cosq": cosq[role], "sinq": sinq[role],
            "maskf": masks[role], "cst": cst,
        })

    nc = build_program()
    res = run_bass_kernel_spmd(nc, in_maps, core_ids=list(range(8)))

    out = np.empty((B, T, C), np.float32)
    for core in range(8):
        b, role = core // 2, core % 2
        rT = np.asarray(res.results[core]["OUT"]).T
        for s, cq in enumerate(ROLE_CHUNKS[role]):
            out[b, cq * CH:(cq + 1) * CH, :] = rT[s * CH:(s + 1) * CH, :]
    if DBG["stage"] is not None:
        kernel._dbg = res
    return out


# version 03
# speedup vs baseline: 1.4965x; 1.1425x over previous
import numpy as np
import ml_dtypes
from contextlib import ExitStack

import concourse.bass as bass
import concourse.mybir as mybir
from concourse.bass_utils import run_bass_kernel_spmd

F32 = mybir.dt.float32
BF16 = mybir.dt.bfloat16
AF = mybir.ActivationFunctionType
ALU = mybir.AluOpType
AX = mybir.AxisListType

B, T, C = 4, 4096, 4096
H, G, D = 32, 8, 128
CH = 512
TOK = 2048
NFI = C // 128
ROLE_CHUNKS = {0: [0, 3, 4, 7], 1: [1, 2, 5, 6]}
CCOUNT = [2, 4, 6, 8]
RSQD = 1.0 / np.sqrt(128.0)
EXP_SHIFT = -8.0

DBG = {"stage": None}

ENGS = ("sync", "tensor", "scalar", "vector", "gpsimd")


class Sem:
    def __init__(self, h):
        self.h = h
        self.n = 0


class Plan:

    def __init__(self, block):
        self.block = block
        self.ops = {e: [] for e in ENGS}

    def do(self, eng, fn):
        self.ops[eng].append(fn)

    def op(self, eng, meth, *a, inc=None, inck=1, **kw):
        if inc is not None:
            inc.n += inck
            s, k = inc.h, inck
            self.ops[eng].append(
                lambda e, m=meth, a=a, kw=kw, s=s, k=k: getattr(e, m)(*a, **kw).then_inc(s, k))
        else:
            self.ops[eng].append(
                lambda e, m=meth, a=a, kw=kw: getattr(e, m)(*a, **kw))

    def w(self, eng, sem, val):
        if val > 0:
            self.ops[eng].append(lambda e, s=sem.h, v=val: e.wait_ge(s, v))

    def flush(self):
        for eng in ENGS:
            fns = self.ops[eng]
            if fns:
                getattr(self.block, eng)(lambda e, fs=tuple(fns): [f(e) for f in fs])
            self.ops[eng] = []


def barrier(p, sems):
    sb = sems["bar"]
    for k in ("in", "ld0", "ld1", "ld2", "ld3"):
        p.w("sync", sems[k], sems[k].n)
    for k in ("gi", "gl0", "gl1"):
        p.w("gpsimd", sems[k], sems[k].n)
    for k in ("st", "sc0", "sc1", "sc2", "sc3"):
        p.w("scalar", sems[k], sems[k].n)
    for eng in ENGS:
        sb.n += 1
        p.do(eng, lambda e, s=sb.h: e.nop().then_inc(s, 1))
    tgt = sb.n
    for eng in ENGS:
        p.w(eng, sb, tgt)


def build_program():
    nc = bass.Bass("TRN2", target_bir_lowering=False)

    def inp(name, shp):
        return nc.dram_tensor(name, shp, F32, kind="ExternalInput").ap()

    xT_own = inp("xT_own", [C, TOK])
    xT_full = inp("xT_full", [C, T])
    wq = inp("wq", [C, C])
    wk = inp("wk", [C, G * D])
    wv = inp("wv", [C, G * D])
    wo = inp("wo", [C, C])
    bo_t = inp("bo_t", [128, NFI])
    worgT = inp("worgT", [C, C])
    down = inp("down", [C, 512])
    up = inp("up", [512, C])
    wqtf = inp("wqtf", [C, 128])
    keysT = inp("keysT", [128, 8])
    i8 = inp("i8", [8, 8])
    ind = inp("ind", [8, 512])
    cosk = inp("cosk", [64, T])
    sink = inp("sink", [64, T])
    cosq = inp("cosq", [64, TOK])
    sinq = inp("sinq", [64, TOK])
    maskf = inp("maskf", [128, 4 * 2 * 4 * CH])
    cst = inp("cst", [128, 8])
    OUT = nc.dram_tensor("OUT", [C, TOK], F32, kind="ExternalOutput").ap()

    ksc = nc.dram_tensor("ksc", [G * D, T], BF16, kind="Internal").ap()
    vsc = nc.dram_tensor("vsc", [T, G * D], BF16, kind="Internal").ap()
    qsc = nc.dram_tensor("qsc", [C, TOK], BF16, kind="Internal").ap()
    asc = nc.dram_tensor("asc", [C, TOK], BF16, kind="Internal").ap()
    rsc = nc.dram_tensor("rsc", [C, TOK], F32, kind="Internal").ap()
    ysc = nc.dram_tensor("ysc", [C, TOK], F32, kind="Internal").ap()
    ccin = nc.dram_tensor("ccin", [1, 128], F32, kind="Internal")
    ccout = nc.dram_tensor("ccout", [2, 128], F32, kind="Internal")

    dbg_outs = {}
    stage = DBG["stage"]
    if stage is not None:
        for nm, src, shp, dt in [
            ("dbg_k", ksc, [G * D, T], BF16), ("dbg_v", vsc, [T, G * D], BF16),
            ("dbg_q", qsc, [C, TOK], BF16), ("dbg_a", asc, [C, TOK], BF16),
            ("dbg_r", rsc, [C, TOK], F32), ("dbg_y", ysc, [C, TOK], F32),
        ]:
            dbg_outs[nm] = (nc.dram_tensor(nm, shp, dt, kind="ExternalOutput").ap(), src)

    with ExitStack() as top:
        ec = top.enter_context
        sems = {k: Sem(ec(nc.semaphore("s_" + k)))
                for k in ("bar", "in", "gi", "pe", "ac", "dv", "st", "cc",
                          "ld0", "ld1", "ld2", "ld3",
                          "gl0", "gl1", "sc0", "sc1", "sc2", "sc3")}
        s_in, s_gi = sems["in"], sems["gi"]
        s_pe, s_ac, s_dv, s_st = sems["pe"], sems["ac"], sems["dv"], sems["st"]
        LD = [sems["ld%d" % i] for i in range(4)]
        GL = [sems["gl%d" % i] for i in range(2)]
        SC = [sems["sc%d" % i] for i in range(4)]

        cst_b = ec(nc.sbuf_tensor("cst_b", [128, 8], F32))
        bo_b = ec(nc.sbuf_tensor("bo_b", [128, NFI], F32))
        ones_c = ec(nc.sbuf_tensor("ones_c", [128, 1], BF16))
        ones_r = ec(nc.sbuf_tensor("ones_r", [1, 128], BF16))
        ones_1 = ec(nc.sbuf_tensor("ones_1", [1, 1], BF16))
        i8_b = ec(nc.sbuf_tensor("i8_b", [8, 8], BF16))
        ind_b = ec(nc.sbuf_tensor("ind_b", [8, 512], BF16))
        keysT_b = ec(nc.sbuf_tensor("keysT_b", [128, 8], BF16))
        wqtf_b = ec(nc.sbuf_tensor("wqtf_b", [128, NFI * 128], BF16))
        ysum = ec(nc.sbuf_tensor("ysum", [128, NFI * 4], F32))
        ssum = ec(nc.sbuf_tensor("ssum", [1, TOK], F32))
        ssq = ec(nc.sbuf_tensor("ssq", [1, TOK], F32))
        c_bc = ec(nc.sbuf_tensor("c_bc", [128, 4], F32))
        block = ec(nc.Block())
        p = Plan(block)

        p.op("sync", "dma_start", inc=s_in, inck=16, out=cst_b[:, :], in_=cst)
        p.op("sync", "dma_start", inc=s_in, inck=16, out=bo_b[:, :], in_=bo_t)
        p.op("gpsimd", "dma_start", inc=s_gi, inck=16, out=i8_b[:, :], in_=i8)
        p.op("gpsimd", "dma_start", inc=s_gi, inck=16, out=ind_b[:, :], in_=ind)
        p.op("gpsimd", "dma_start", inc=s_gi, inck=16, out=keysT_b[:, :], in_=keysT)
        p.op("gpsimd", "dma_start", inc=s_gi, inck=16,
             out=wqtf_b[:, :].rearrange("p (c k) -> p c k", c=NFI),
             in_=wqtf.rearrange("(c p) k -> p c k", p=128))
        p.op("vector", "memset", ones_c[:, :], 1.0)
        p.op("vector", "memset", ones_r[:, :], 1.0)
        p.op("vector", "memset", ones_1[:, :], 1.0)
        p.op("vector", "memset", ysum[:, :], 0.0)
        p.op("vector", "memset", ssum[:, :], 0.0)
        p.op("vector", "memset", ssq[:, :], 0.0)
        barrier(p, sems)
        p.flush()

        with ExitStack() as stA:
            eA = stA.enter_context

            evb = [eA(nc.sbuf_tensor(f"evb{i}", [128, 512], BF16)) for i in range(4)]
            rt_a = eA(nc.sbuf_tensor("rt_a", [64, 512], F32))
            rt_b2 = eA(nc.sbuf_tensor("rt_b2", [64, 512], F32))
            ps = [eA(nc.psum_tensor(f"ps{i}", [128, 512], F32)) for i in range(2)]
            stVK = ExitStack()
            eVK = stVK.enter_context
            w_big = eVK(nc.sbuf_tensor("w_big", [128, NFI * 1024], BF16))
            xb = [eVK(nc.sbuf_tensor(f"xb{i}", [128, NFI * CH], BF16)) for i in range(2)]
            cos_b = eVK(nc.sbuf_tensor("cos_b", [64, T], F32))
            sin_b = eVK(nc.sbuf_tensor("sin_b", [64, T], F32))

            p.op("sync", "dma_start", inc=s_in, inck=16, out=cos_b[:, :], in_=cosk)
            p.op("sync", "dma_start", inc=s_in, inck=16, out=sin_b[:, :], in_=sink)

            def stream_proj(kind):
                w_in = wv if kind == "v" else wk
                p.op("gpsimd", "dma_start", inc=s_gi, inck=16,
                     out=w_big[:, :].rearrange("p (c v) -> p c v", c=NFI),
                     in_=w_in.rearrange("(c p) v -> p c v", p=128))
                w_mark = s_gi.n
                x_marks = {}
                grp_done = {}
                ev_free = {}
                ps_free = {}
                tile_i = 0
                for tg in range(8):
                    if tg >= 2:
                        p.w("gpsimd", s_pe, grp_done[(tg - 2, "all")])
                    p.op("gpsimd", "dma_start", inc=GL[tg % 2], inck=16,
                         out=xb[tg % 2][:, :].rearrange("p (c t) -> p c t", c=NFI),
                         in_=xT_full[:, tg * CH:(tg + 1) * CH]
                         .rearrange("(c p) t -> p c t", p=128))
                    x_marks[tg] = GL[tg % 2].n
                    p.w("tensor", s_gi, w_mark)
                    p.w("tensor", GL[tg % 2], x_marks[tg])
                    ntile = 8
                    for j in range(ntile):
                        rot = tile_i % 2
                        if tile_i >= 2:
                            p.w("tensor", s_ac if kind == "v" else s_dv, ps_free[rot])
                        if kind == "v":
                            tsub, vh = j // 2, j % 2
                            for c in range(NFI):
                                p.op("tensor", "matmul", ps[rot][:, :],
                                     xb[tg % 2][:, c * CH + tsub * 128: c * CH + (tsub + 1) * 128],
                                     w_big[:, c * 1024 + vh * 512: c * 1024 + (vh + 1) * 512],
                                     start=(c == 0), stop=(c == NFI - 1),
                                     inc=s_pe if c == NFI - 1 else None)
                        else:
                            g = j
                            for c in range(NFI):
                                p.op("tensor", "matmul", ps[rot][:, :],
                                     w_big[:, c * 1024 + g * 128: c * 1024 + (g + 1) * 128],
                                     xb[tg % 2][:, c * CH:(c + 1) * CH],
                                     start=(c == 0), stop=(c == NFI - 1),
                                     inc=s_pe if c == NFI - 1 else None)
                        mm_mark = s_pe.n
                        erot = tile_i % 4
                        if tile_i >= 4:
                            eng = "scalar" if kind == "v" else "vector"
                            p.w(eng, SC[erot], ev_free[erot])
                        if kind == "v":
                            p.w("scalar", s_pe, mm_mark)
                            p.op("scalar", "copy", evb[erot][:, :], ps[rot][:, :], inc=s_ac)
                            ps_free[rot] = s_ac.n
                            p.w("scalar", s_ac, s_ac.n)
                            p.op("scalar", "dma_start", inc=SC[erot], inck=16,
                                 out=vsc[tg * CH + tsub * 128: tg * CH + (tsub + 1) * 128,
                                         vh * 512:(vh + 1) * 512],
                                 in_=evb[erot][:, :])
                        else:
                            p.w("vector", s_pe, mm_mark)
                            cs_ = cos_b[:, tg * CH:(tg + 1) * CH]
                            sn_ = sin_b[:, tg * CH:(tg + 1) * CH]
                            pr, pi = ps[rot][0:64, :], ps[rot][64:128, :]
                            p.op("vector", "tensor_mul", rt_a[:, :], pr, cs_)
                            p.op("vector", "tensor_mul", rt_b2[:, :], pi, sn_)
                            p.op("vector", "tensor_sub", evb[erot][0:64, :], rt_a[:, :], rt_b2[:, :])
                            p.op("vector", "tensor_mul", rt_a[:, :], pr, sn_)
                            p.op("vector", "tensor_mul", rt_b2[:, :], pi, cs_)
                            p.op("vector", "tensor_add", evb[erot][64:128, :],
                                 rt_a[:, :], rt_b2[:, :], inc=s_dv)
                            ps_free[rot] = s_dv.n
                            p.w("scalar", s_dv, s_dv.n)
                            p.op("scalar", "dma_start", inc=SC[erot], inck=16,
                                 out=ksc[g * 128:(g + 1) * 128, tg * CH:(tg + 1) * CH],
                                 in_=evb[erot][:, :])
                        ev_free[erot] = SC[erot].n
                        tile_i += 1
                    grp_done[(tg, "all")] = s_pe.n

            p.w("tensor", s_in, s_in.n)
            stream_proj("v")
            barrier(p, sems)
            stream_proj("k")
            barrier(p, sems)
            p.flush()
            stVK.close()

            with ExitStack() as stQ:
                eQ = stQ.enter_context
                xob = eQ(nc.sbuf_tensor("xob", [128, NFI * TOK], BF16))
                wqb = [eQ(nc.sbuf_tensor(f"wqb{i}", [128, NFI * 128], BF16)) for i in range(2)]
                cq_b = eQ(nc.sbuf_tensor("cq_b", [64, TOK], F32))
                sq_b = eQ(nc.sbuf_tensor("sq_b", [64, TOK], F32))

                p.op("sync", "dma_start", inc=s_in, inck=16, out=cq_b[:, :], in_=cosq)
                p.op("sync", "dma_start", inc=s_in, inck=16, out=sq_b[:, :], in_=sinq)
                for to in range(4):
                    p.op("gpsimd", "dma_start", inc=s_gi, inck=16,
                         out=xob[:, :].rearrange("p (c t) -> p c t", c=NFI)[:, :, to * CH:(to + 1) * CH],
                         in_=xT_own[:, to * CH:(to + 1) * CH].rearrange("(c p) t -> p c t", p=128))
                xo_mark = s_gi.n
                p.w("tensor", s_gi, xo_mark)
                p.w("tensor", s_in, s_in.n)
                wq_marks = {}
                hd_done = {}
                ev_free = {}
                ps_free = {}
                tile_i = 0
                for hd in range(H):
                    if hd >= 2:
                        p.w("gpsimd", s_pe, hd_done[hd - 2])
                    p.op("gpsimd", "dma_start", inc=GL[hd % 2], inck=16,
                         out=wqb[hd % 2][:, :].rearrange("p (c k) -> p c k", c=NFI),
                         in_=wq[:, hd * 128:(hd + 1) * 128].rearrange("(c p) k -> p c k", p=128))
                    wq_marks[hd] = GL[hd % 2].n
                    p.w("tensor", GL[hd % 2], wq_marks[hd])
                    for to in range(4):
                        rot = tile_i % 2
                        if tile_i >= 2:
                            p.w("tensor", s_dv, ps_free[rot])
                        for c in range(NFI):
                            p.op("tensor", "matmul", ps[rot][:, :],
                                 wqb[hd % 2][:, c * 128:(c + 1) * 128],
                                 xob[:, c * TOK + to * CH: c * TOK + (to + 1) * CH],
                                 start=(c == 0), stop=(c == NFI - 1),
                                 inc=s_pe if c == NFI - 1 else None)
                        mm_mark = s_pe.n
                        erot = tile_i % 4
                        if tile_i >= 4:
                            p.w("vector", SC[erot], ev_free[erot])
                        p.w("vector", s_pe, mm_mark)
                        cs_ = cq_b[:, to * CH:(to + 1) * CH]
                        sn_ = sq_b[:, to * CH:(to + 1) * CH]
                        pr, pi = ps[rot][0:64, :], ps[rot][64:128, :]
                        p.op("vector", "tensor_mul", rt_a[:, :], pr, cs_)
                        p.op("vector", "tensor_mul", rt_b2[:, :], pi, sn_)
                        p.op("vector", "tensor_sub", evb[erot][0:64, :], rt_a[:, :], rt_b2[:, :])
                        p.op("vector", "tensor_mul", rt_a[:, :], pr, sn_)
                        p.op("vector", "tensor_mul", rt_b2[:, :], pi, cs_)
                        p.op("vector", "tensor_add", evb[erot][64:128, :],
                             rt_a[:, :], rt_b2[:, :], inc=s_dv)
                        ps_free[rot] = s_dv.n
                        p.w("scalar", s_dv, s_dv.n)
                        p.op("scalar", "dma_start", inc=SC[erot], inck=16,
                             out=qsc[hd * 128:(hd + 1) * 128, to * CH:(to + 1) * CH],
                             in_=evb[erot][:, :])
                        ev_free[erot] = SC[erot].n
                        tile_i += 1
                    hd_done[hd] = s_pe.n
                barrier(p, sems)
                p.flush()

        if stage in ("kv", "q"):
            emit_debug(p, sems, dbg_outs, stage)
            p.flush()
            return nc

        with ExitStack() as st4:
            e4 = st4.enter_context
            mask_b = e4(nc.sbuf_tensor("mask_b", [128, 4 * 2 * 4 * CH], BF16))
            kgb = [e4(nc.sbuf_tensor(f"kgb{i}", [128, T], BF16)) for i in range(2)]
            vgb = [e4(nc.sbuf_tensor(f"vgb{i}", [128, 32 * 130], BF16)) for i in range(2)]
            qhb = [e4(nc.sbuf_tensor(f"qhb{i}", [128, TOK], BF16)) for i in range(2)]
            p_sb = [e4(nc.sbuf_tensor(f"p_sb{i}", [128, 512], BF16)) for i in range(4)]
            rsum = e4(nc.sbuf_tensor("rsum", [1, 512], F32))
            rsum0 = e4(nc.sbuf_tensor("rsum0", [1, 512], F32))
            rhi = e4(nc.sbuf_tensor("rhi", [1, 512], BF16))
            rhf = e4(nc.sbuf_tensor("rhf", [1, 512], F32))
            rlf = e4(nc.sbuf_tensor("rlf", [1, 512], F32))
            rlo = e4(nc.sbuf_tensor("rlo", [1, 512], BF16))
            rt_sb = [e4(nc.sbuf_tensor(f"rt_sb{i}", [128, 512], F32)) for i in range(2)]
            at_sb = [e4(nc.sbuf_tensor(f"at_sb{i}", [128, 512], BF16)) for i in range(2)]
            st_ps = e4(nc.psum_tensor("st_ps", [128, 1536], F32))
            pva_ps = [e4(nc.psum_tensor(f"pva_ps{i}", [128, 512], F32)) for i in range(2)]
            pvb_ps = [e4(nc.psum_tensor(f"pvb_ps{i}", [128, 512], F32)) for i in range(2)]
            rt_ps = e4(nc.psum_tensor("rt_ps", [128, 512], F32))

            p.op("gpsimd", "dma_start", inc=s_gi, inck=16, out=mask_b[:, :], in_=maskf)
            mask_mark = s_gi.n
            for i in range(2):
                vb0 = vgb[i][:, :].rearrange("p (ks c) -> p ks c", c=130)
                p.op("vector", "memset", vb0[:, :, 64:65], 1.0)
                p.op("vector", "memset", vb0[:, :, 129:130], 1.0, inc=s_dv if i == 1 else None)
            ones_cols_mark = s_dv.n

            def load_kv(g, pe_wait):
                sem = LD[g % 2]
                p.w("sync", s_pe, pe_wait)
                p.op("sync", "dma_start", inc=sem, inck=16,
                     out=kgb[g % 2][:, :], in_=ksc[g * 128:(g + 1) * 128, :])
                vv = vsc[:, g * 128:(g + 1) * 128].rearrange("(ks p) d -> p ks d", p=128)
                vb = vgb[g % 2][:, :].rearrange("p (ks c) -> p ks c", c=130)
                for lo, hi in ((0, 16), (16, 32)):
                    p.op("sync", "dma_start", inc=sem, inck=16,
                         out=vb[:, lo:hi, 0:64], in_=vv[:, lo:hi, 0:64])
                    p.op("sync", "dma_start", inc=sem, inck=16,
                         out=vb[:, lo:hi, 65:129], in_=vv[:, lo:hi, 64:128])
                return (sem, sem.n)

            def load_q(h, pe_wait):
                sem = LD[2 + h % 2]
                p.w("sync", s_pe, pe_wait)
                p.op("sync", "dma_start", inc=sem, inck=16,
                     out=qhb[h % 2][:, :], in_=qsc[h * 128:(h + 1) * 128, :])
                return (sem, sem.n)

            kv_mark = {}
            q_mark = {}
            g_done = {-1: 0, -2: 0}
            h_done = {-1: 0, -2: 0}
            kv_mark[0] = load_kv(0, 0)
            q_mark[0] = load_q(0, 0)
            exp_mark = {}
            pair_consumed = {}
            at_st_mark = {0: 0, 1: 0}
            mask_dv_mark = {}

            slot_i = 0
            pending_tail = [None]
            tail_state = {"rt_cp": 0, "rt_mm": 0, "at_ev": {0: 0, 1: 0}}

            def make_tail(s, h, pv_done, sp):
                def tail():
                    p.w("vector", s_pe, pv_done)
                    p.w("vector", s_pe, tail_state["rt_mm"])
                    p.op("vector", "reciprocal", rsum[:, :],
                         pva_ps[sp][64:65, :], inc=s_dv)
                    p.w("vector", s_dv, s_dv.n)
                    p.op("vector", "tensor_copy", rhi[:, :], rsum[:, :])
                    p.op("vector", "tensor_copy", rhf[:, :], rhi[:, :])
                    p.op("vector", "tensor_sub", rlf[:, :], rsum[:, :], rhf[:, :])
                    p.op("vector", "tensor_copy", rlo[:, :], rlf[:, :], inc=s_dv)
                    hl_mark = s_dv.n
                    p.w("tensor", s_dv, hl_mark)
                    p.w("tensor", s_ac, tail_state["rt_cp"])
                    p.op("tensor", "matmul", rt_ps[:, :], ones_r[:, :], rhi[:, :],
                         start=True, stop=False)
                    p.op("tensor", "matmul", rt_ps[:, :], ones_r[:, :], rlo[:, :],
                         start=False, stop=True, inc=s_pe)
                    tail_state["rt_mm"] = s_pe.n
                    p.w("scalar", s_pe, tail_state["rt_mm"])
                    p.op("scalar", "copy", rt_sb[sp][:, :], rt_ps[:, :], inc=s_ac)
                    tail_state["rt_cp"] = s_ac.n
                    p.w("vector", s_ac, tail_state["rt_cp"])
                    p.w("vector", SC[sp], at_st_mark[sp])
                    p.op("vector", "tensor_mul", at_sb[sp][0:64, :],
                         pva_ps[sp][0:64, :], rt_sb[sp][0:64, :])
                    p.op("vector", "tensor_mul", at_sb[sp][64:128, :],
                         pvb_ps[sp][0:64, :], rt_sb[sp][64:128, :], inc=s_dv)
                    tail_state["at_ev"][sp] = s_dv.n
                    p.w("scalar", s_dv, tail_state["at_ev"][sp])
                    p.op("scalar", "dma_start", inc=SC[sp], inck=16,
                         out=asc[h * 128:(h + 1) * 128, s * CH:(s + 1) * CH],
                         in_=at_sb[sp][:, :])
                    at_st_mark[sp] = SC[sp].n
                return tail

            for g in range(G):
                if g + 1 < G:
                    kv_mark[g + 1] = load_kv(g + 1, g_done[g - 1])
                for hh in range(4):
                    h = g * 4 + hh
                    if h + 1 < H:
                        hq = h + 1
                        q_mark[hq] = load_q(hq, h_done[h - 1])
                    p.w("tensor", kv_mark[g][0], kv_mark[g][1])
                    p.w("tensor", q_mark[h][0], q_mark[h][1])
                    if h == 0:
                        p.w("vector", s_gi, mask_mark)
                        p.w("tensor", s_dv, ones_cols_mark)
                    for s in range(4):
                        cs = CCOUNT[s]
                        nks = 4 * cs
                        sp = s % 2
                        qs = qhb[h % 2][:, s * CH:(s + 1) * CH]
                        vsl = vgb[g % 2]

                        def emit_consume(kk, first, last):
                            ck = kk // 4
                            masked = ck >= cs - 2
                            p.w("tensor", s_ac, exp_mark[(slot_i, kk)])
                            if masked:
                                p.w("tensor", s_dv, mask_dv_mark[(slot_i, kk)])
                            if first:
                                p.w("tensor", s_dv, tail_state["at_ev"][sp])
                            pt = p_sb[kk % 4][:, :]
                            p.op("tensor", "matmul", pva_ps[sp][0:65, :],
                                 vsl[:, kk * 130: kk * 130 + 65], pt,
                                 start=first, stop=last)
                            p.op("tensor", "matmul", pvb_ps[sp][0:65, :],
                                 vsl[:, kk * 130 + 65: kk * 130 + 130], pt,
                                 start=first, stop=last,
                                 inc=s_pe if last else None)
                            pair_consumed[(slot_i, kk)] = s_pe.n

                        for kk in range(nks):
                            if kk >= 3:
                                p.w("tensor", s_ac, exp_mark[(slot_i, kk - 3)])
                            p.op("tensor", "matmul",
                                 st_ps[:, (kk % 3) * 512:(kk % 3) * 512 + 512],
                                 kgb[g % 2][:, kk * 128:(kk + 1) * 128], qs,
                                 start=True, stop=True, inc=s_pe)
                            st_mark = s_pe.n
                            p.w("scalar", s_pe, st_mark)
                            if kk >= 4:
                                p.w("scalar", s_pe, pair_consumed[(slot_i, kk - 4)])
                            p.op("scalar", "activation",
                                 p_sb[kk % 4][:, :],
                                 st_ps[:, (kk % 3) * 512:(kk % 3) * 512 + 512],
                                 AF.Exp, bias=cst_b[:, 0:1], scale=RSQD, inc=s_ac)
                            exp_mark[(slot_i, kk)] = s_ac.n
                            ck = kk // 4
                            masked = ck >= cs - 2
                            if masked:
                                which = ck - (cs - 2)
                                base = ((s * 2 + which) * 4 + (kk % 4)) * CH
                                p.w("vector", s_ac, exp_mark[(slot_i, kk)])
                                p.op("vector", "tensor_mul", p_sb[kk % 4][:, :],
                                     p_sb[kk % 4][:, :], mask_b[:, base: base + 512],
                                     inc=s_dv)
                                mask_dv_mark[(slot_i, kk)] = s_dv.n
                            if kk == 4 and pending_tail[0] is not None:
                                pending_tail[0]()
                                pending_tail[0] = None
                            if kk >= 2:
                                emit_consume(kk - 2, first=(kk == 2), last=False)
                        emit_consume(nks - 2, first=False, last=False)
                        emit_consume(nks - 1, first=False, last=True)
                        pv_done = s_pe.n
                        pending_tail[0] = make_tail(s, h, pv_done, sp)
                        slot_i += 1
                    h_done[h] = s_pe.n
                g_done[g] = s_pe.n
            if pending_tail[0] is not None:
                pending_tail[0]()
                pending_tail[0] = None
            barrier(p, sems)
            p.flush()

        if stage == "attn":
            emit_debug(p, sems, dbg_outs, stage)
            p.flush()
            return nc

        with ExitStack() as st5:
            e5 = st5.enter_context
            attn_b = e5(nc.sbuf_tensor("attn_b", [128, NFI * TOK], BF16))
            wob = [e5(nc.sbuf_tensor(f"wob{i}", [128, NFI * 128], BF16)) for i in range(2)]
            xo_f = [e5(nc.sbuf_tensor(f"xo_f{i}", [128, 512], F32)) for i in range(3)]
            r_f = [e5(nc.sbuf_tensor(f"r_f{i}", [128, 512], F32)) for i in range(3)]
            r_bb = [e5(nc.sbuf_tensor(f"r_bb{i}", [128, 512], BF16)) for i in range(2)]
            r2_bb = [e5(nc.sbuf_tensor(f"r2_bb{i}", [128, 512], BF16)) for i in range(2)]
            ps_wo = [e5(nc.psum_tensor(f"ps_wo{i}", [128, 512], F32)) for i in range(2)]
            st_s = e5(nc.psum_tensor("st_s", [128, 512], F32))
            st_q = e5(nc.psum_tensor("st_q", [128, 512], F32))

            p.op("sync", "dma_start", inc=s_in, inck=16,
                 out=attn_b[:, :].rearrange("p (hh q) -> p hh q", hh=NFI),
                 in_=asc.rearrange("(hh p) q -> p hh q", p=128))
            attn_mark = s_in.n
            emit_proj_res_stats(
                p, sems, nc, src_res=xT_own, res_scale_bias=True,
                w_dram=wo, w_bufs=wob, act_b=attn_b, xo_f=xo_f, r_f=r_f,
                r_bb=r_bb, r2_bb=r2_bb, ps_rot=ps_wo, st_s=st_s, st_q=st_q,
                ssum=ssum, ssq=ssq, bo_b=bo_b, out_dram=rsc,
                pre_marks=(attn_mark, 0), ones_c=ones_c, extra=None)
            barrier(p, sems)
            p.flush()

        if stage == "wo":
            emit_debug(p, sems, dbg_outs, stage)
            p.flush()
            return nc

        with ExitStack() as st6:
            e6 = st6.enter_context
            ps_a = [e6(nc.psum_tensor(f"ps_a{i}", [128, 512], F32)) for i in range(2)]
            stYH = ExitStack()
            st6.enter_context(stYH)
            y_b = stYH.enter_context(nc.sbuf_tensor("y_b", [128, NFI * TOK], BF16))

            with ExitStack() as st6i:
                e6i = st6i.enter_context
                mh = {}
                for nm in ("mh", "ml", "rh", "rl"):
                    mh[nm] = e6i(nc.sbuf_tensor("ln_" + nm, [1, TOK], BF16))
                tmp1 = e6i(nc.sbuf_tensor("ln_t1", [1, TOK], F32))
                tmp2 = e6i(nc.sbuf_tensor("ln_t2", [1, TOK], F32))
                M_sb = [e6i(nc.sbuf_tensor(f"M_sb{i}", [128, 512], F32)) for i in range(2)]
                R_sb = [e6i(nc.sbuf_tensor(f"R_sb{i}", [128, 512], F32)) for i in range(2)]
                rr_f = [e6i(nc.sbuf_tensor(f"rr_f{i}", [128, 512], F32)) for i in range(3)]
                yy_f = [e6i(nc.sbuf_tensor(f"yy_f{i}", [128, 512], F32)) for i in range(3)]
                ysp = e6i(nc.sbuf_tensor("ysp", [128, 1], F32))
                br_ps = e6i(nc.psum_tensor("br_ps", [128, 1024], F32))

                emit_ln_finalize(p, sems, ssum, ssq, tmp1, tmp2, mh)
                emit_ln_apply(p, sems, rsc, mh, tmp1, M_sb, R_sb, rr_f, yy_f,
                              br_ps, ones_r, out_dram=ysc, y_b=y_b, ysp=ysp, ysum=ysum,
                              out_dram2=None, OUT=None)
                barrier(p, sems)
                p.flush()

            if stage == "ln1":
                ys_o = nc.dram_tensor("dbg_ysum", [128, NFI * 4], F32,
                                      kind="ExternalOutput").ap()
                p.op("scalar", "dma_start", inc=s_st, inck=16,
                     out=ys_o, in_=ysum[:, :])
                p.w("scalar", s_st, s_st.n)
                emit_debug(p, sems, dbg_outs, stage)
                p.flush()
                return nc

            h_b = stYH.enter_context(nc.sbuf_tensor("h_b", [128, 4 * TOK], BF16))
            with ExitStack() as st7:
                e7 = st7.enter_context
                dnb = [e7(nc.sbuf_tensor(f"dnb{i}", [128, NFI * 128], BF16)) for i in range(2)]
                ysb = e7(nc.sbuf_tensor("cc_ysb", [128, NFI * 4], BF16))
                q4 = e7(nc.sbuf_tensor("cc_q4", [128, 4], F32))
                q_sb = e7(nc.sbuf_tensor("cc_qsb", [128, 1], F32))
                qa = e7(nc.sbuf_tensor("cc_qa", [1, 128], F32))
                qb = e7(nc.sbuf_tensor("cc_qb", [1, 128], F32))
                qt = e7(nc.sbuf_tensor("cc_qt", [1, 128], F32))
                qt_bf = e7(nc.sbuf_tensor("cc_qtb", [1, 128], BF16))
                qc_bf = e7(nc.sbuf_tensor("cc_qcb", [128, 1], BF16))
                sc_bf = e7(nc.sbuf_tensor("cc_scb", [8, 1], BF16))
                e8s = e7(nc.sbuf_tensor("cc_e8", [1, 8], F32))
                se = e7(nc.sbuf_tensor("cc_se", [1, 1], F32))
                rse = e7(nc.sbuf_tensor("cc_rse", [1, 1], F32))
                cf = e7(nc.sbuf_tensor("cc_cf", [1, 8], F32))
                cf_bf = e7(nc.sbuf_tensor("cc_cfb", [1, 8], BF16))
                ct_bf = e7(nc.sbuf_tensor("cc_ctb", [8, 1], BF16))

                cps = e7(nc.psum_tensor("cps", [128, 512], F32))
                cc_tiles = (ysb, q4, q_sb, qa, qb, qt, qt_bf, qc_bf, sc_bf,
                            e8s, se, rse, cf, cf_bf, ct_bf, ccin, ccout)
                emit_coeffs_pre(p, sems, ysum, wqtf_b, cc_tiles, cps)
                dn_marks = {}
                ec_done = {}
                ps_free = {}
                tile_i = 0
                for ecx in range(4):
                    if ecx >= 2:
                        p.w("gpsimd", s_pe, ec_done[ecx - 2])
                    p.op("gpsimd", "dma_start", inc=GL[ecx % 2], inck=16,
                         out=dnb[ecx % 2][:, :].rearrange("p (c k) -> p c k", c=NFI),
                         in_=down[:, ecx * 128:(ecx + 1) * 128].rearrange("(c p) k -> p c k", p=128))
                    dn_marks[ecx] = GL[ecx % 2].n
                    p.w("tensor", GL[ecx % 2], dn_marks[ecx])
                    for ts in range(4):
                        rot = tile_i % 2
                        if tile_i >= 2:
                            p.w("tensor", s_ac, ps_free[rot])
                        for c in range(NFI):
                            p.op("tensor", "matmul", ps_a[rot][:, :],
                                 dnb[ecx % 2][:, c * 128:(c + 1) * 128],
                                 y_b[:, c * TOK + ts * CH: c * TOK + (ts + 1) * CH],
                                 start=(c == 0), stop=(c == NFI - 1),
                                 inc=s_pe if c == NFI - 1 else None)
                        p.w("scalar", s_pe, s_pe.n)
                        p.op("scalar", "copy",
                             h_b[:, ecx * TOK + ts * CH: ecx * TOK + (ts + 1) * CH],
                             ps_a[rot][:, :], inc=s_ac)
                        ps_free[rot] = s_ac.n
                        tile_i += 1
                    ec_done[ecx] = s_pe.n
                emit_coeffs_post(p, sems, keysT_b, i8_b, ind_b, ones_1,
                                 cc_tiles, c_bc, cps)
                barrier(p, sems)
                p.flush()

            p.op("vector", "memset", ssum[:, :], 0.0)
            p.op("vector", "memset", ssq[:, :], 0.0)
            with ExitStack() as st8:
                e8 = st8.enter_context
                wgb = [e8(nc.sbuf_tensor(f"wgb{i}", [128, NFI * 128], BF16)) for i in range(2)]
                upb = [e8(nc.sbuf_tensor(f"upb{i}", [128, 4 * 128], BF16)) for i in range(2)]
                yo_f = [e8(nc.sbuf_tensor(f"yo_f{i}", [128, 512], F32)) for i in range(3)]
                r_f = [e8(nc.sbuf_tensor(f"r8_f{i}", [128, 512], F32)) for i in range(3)]
                r_bb = [e8(nc.sbuf_tensor(f"r8_bb{i}", [128, 512], BF16)) for i in range(2)]
                r2_bb = [e8(nc.sbuf_tensor(f"r8_2bb{i}", [128, 512], BF16)) for i in range(2)]
                st_s = e8(nc.psum_tensor("st8_s", [128, 512], F32))
                st_q = e8(nc.psum_tensor("st8_q", [128, 512], F32))

                emit_proj_res_stats(
                    p, sems, nc, src_res=ysc, res_scale_bias=False,
                    w_dram=worgT, w_bufs=wgb, act_b=y_b, xo_f=yo_f, r_f=r_f,
                    r_bb=r_bb, r2_bb=r2_bb, ps_rot=ps_a, st_s=st_s, st_q=st_q,
                    ssum=ssum, ssq=ssq, bo_b=None, out_dram=rsc,
                    pre_marks=(0, 0), ones_c=ones_c,
                    extra={"up": up, "upb": upb, "h_b": h_b, "c_bc": c_bc})
                barrier(p, sems)
                p.flush()
            stYH.close()

            with ExitStack() as st9:
                e9 = st9.enter_context
                mh = {}
                for nm in ("mh", "ml", "rh", "rl"):
                    mh[nm] = e9(nc.sbuf_tensor("l2_" + nm, [1, TOK], BF16))
                tmp1 = e9(nc.sbuf_tensor("l2_t1", [1, TOK], F32))
                tmp2 = e9(nc.sbuf_tensor("l2_t2", [1, TOK], F32))
                M_sb = [e9(nc.sbuf_tensor(f"M2_sb{i}", [128, 512], F32)) for i in range(2)]
                R_sb = [e9(nc.sbuf_tensor(f"R2_sb{i}", [128, 512], F32)) for i in range(2)]
                rr_f = [e9(nc.sbuf_tensor(f"r9_f{i}", [128, 512], F32)) for i in range(3)]
                yy_f = [e9(nc.sbuf_tensor(f"y9_f{i}", [128, 512], F32)) for i in range(3)]
                br_ps = e9(nc.psum_tensor("br2_ps", [128, 1024], F32))

                emit_ln_finalize(p, sems, ssum, ssq, tmp1, tmp2, mh)
                emit_ln_apply(p, sems, rsc, mh, tmp1, M_sb, R_sb, rr_f, yy_f,
                              br_ps, ones_r, out_dram=None, y_b=None, ysp=None,
                              ysum=None, out_dram2=None, OUT=OUT)
                for k in ("st", "sc0", "sc1", "sc2", "sc3"):
                    p.w("scalar", sems[k], sems[k].n)
                if stage == "ffn":
                    emit_debug(p, sems, dbg_outs, stage)
                p.flush()
    return nc


def emit_debug(p, sems, dbg_outs, stage):
    s_gi = sems["gi"]
    for nm, (out_ap, src) in dbg_outs.items():
        p.op("gpsimd", "dma_start", inc=s_gi, inck=16, out=out_ap, in_=src)
    p.w("gpsimd", s_gi, s_gi.n)


def emit_proj_res_stats(p, sems, nc, *, src_res, res_scale_bias, w_dram, w_bufs,
                        act_b, xo_f, r_f, r_bb, r2_bb, ps_rot, st_s, st_q,
                        ssum, ssq, bo_b, out_dram, pre_marks, ones_c, extra):
    s_in, s_gi = sems["in"], sems["gi"]
    s_pe, s_ac, s_dv, s_st = sems["pe"], sems["ac"], sems["dv"], sems["st"]
    LD = [sems["ld%d" % i] for i in range(4)]
    GL = [sems["gl%d" % i] for i in range(2)]
    SC = [sems["sc%d" % i] for i in range(4)]
    act_mark, _ = pre_marks
    w_marks = {}
    co_done = {}
    ps_free = {}
    xo_marks = {}
    r_free = {}
    stat_dv = 0
    tile_i = 0
    pending_stats = []
    stat_guard = {}
    p.w("tensor", s_in, act_mark)
    for co in range(NFI):
        if co >= 2:
            p.w("gpsimd", s_pe, co_done[co - 2])
        p.op("gpsimd", "dma_start", inc=GL[co % 2], inck=16,
             out=w_bufs[co % 2][:, :].rearrange("p (c k) -> p c k", c=NFI),
             in_=w_dram[:, co * 128:(co + 1) * 128].rearrange("(c p) k -> p c k", p=128))
        if extra is not None:
            p.op("gpsimd", "dma_start", inc=GL[co % 2], inck=16,
                 out=extra["upb"][co % 2][:, :].rearrange("p (e k) -> p e k", e=4),
                 in_=extra["up"][:, co * 128:(co + 1) * 128].rearrange("(e p) k -> p e k", p=128))
        w_marks[co] = GL[co % 2].n
        p.w("tensor", GL[co % 2], w_marks[co])
        if extra is not None:
            p.w("vector", GL[co % 2], w_marks[co])
            if co >= 2:
                p.w("vector", s_pe, co_done[co - 2])
            for ecx in range(4):
                p.op("vector", "tensor_scalar",
                     extra["upb"][co % 2][:, ecx * 128:(ecx + 1) * 128],
                     extra["upb"][co % 2][:, ecx * 128:(ecx + 1) * 128],
                     extra["c_bc"][:, ecx:ecx + 1], None, ALU.mult,
                     inc=s_dv if ecx == 3 else None)
            up_scaled = s_dv.n
            p.w("tensor", s_dv, up_scaled)
        for ts in range(4):
            rot = tile_i % 2
            xrot = tile_i % 3
            if tile_i >= 3:
                p.w("sync", s_dv, r_free.get(("x", xrot), 0))
            p.op("sync", "dma_start", inc=LD[xrot], inck=16,
                 out=xo_f[xrot][:, :],
                 in_=src_res[co * 128:(co + 1) * 128, ts * CH:(ts + 1) * CH])
            xo_marks[tile_i] = LD[xrot].n
            if tile_i >= 2:
                p.w("tensor", s_dv, ps_free[rot])
            for c in range(NFI):
                p.op("tensor", "matmul", ps_rot[rot][:, :],
                     w_bufs[co % 2][:, c * 128:(c + 1) * 128],
                     act_b[:, c * TOK + ts * CH: c * TOK + (ts + 1) * CH],
                     start=(c == 0), stop=(extra is None and c == NFI - 1),
                     inc=(s_pe if (extra is None and c == NFI - 1) else None))
            if extra is not None:
                for ecx in range(4):
                    p.op("tensor", "matmul", ps_rot[rot][:, :],
                         extra["upb"][co % 2][:, ecx * 128:(ecx + 1) * 128],
                         extra["h_b"][:, ecx * TOK + ts * CH: ecx * TOK + (ts + 1) * CH],
                         start=False, stop=(ecx == 3),
                         inc=s_pe if ecx == 3 else None)
            mm_mark = s_pe.n
            if pending_stats:
                pending_stats.pop(0)()
                stat_guard[(tile_i - 1) % 2] = s_pe.n
            p.w("vector", s_pe, mm_mark)
            p.w("vector", LD[xrot], xo_marks[tile_i])
            rrot = tile_i % 3
            if tile_i >= 3:
                p.w("vector", SC[rrot], r_free.get(("r", rrot), 0))
            if res_scale_bias:
                p.op("vector", "scalar_tensor_tensor", r_f[rrot][:, :],
                     ps_rot[rot][:, :], bo_b[:, co:co + 1], xo_f[xrot][:, :],
                     ALU.add, ALU.add)
            else:
                p.op("vector", "tensor_add", r_f[rrot][:, :],
                     ps_rot[rot][:, :], xo_f[xrot][:, :])
            brot = tile_i % 2
            p.w("vector", s_pe, stat_guard.get(brot, 0))
            p.op("vector", "tensor_copy", r_bb[brot][:, :], r_f[rrot][:, :], inc=s_dv)
            ps_free[rot] = s_dv.n
            r_free[("x", xrot)] = s_dv.n
            rb_mark = s_dv.n
            p.w("scalar", s_dv, rb_mark)
            p.op("scalar", "square", r2_bb[brot][:, :], r_bb[brot][:, :], inc=s_ac)
            sq_mark = s_ac.n
            def make_stats(ts=ts, brot=brot, rb_mark=rb_mark, sq_mark=sq_mark):
                def emit():
                    nonlocal stat_dv
                    p.w("tensor", s_dv, stat_dv)
                    p.w("tensor", s_dv, rb_mark)
                    p.w("tensor", s_ac, sq_mark)
                    p.op("tensor", "matmul", st_s[0:1, :], ones_c[:, :],
                         r_bb[brot][:, :], start=True, stop=True)
                    p.op("tensor", "matmul", st_q[0:1, :], ones_c[:, :],
                         r2_bb[brot][:, :], start=True, stop=True, inc=s_pe)
                    stat_pe = s_pe.n
                    p.w("vector", s_pe, stat_pe)
                    p.op("vector", "tensor_add", ssum[0:1, ts * CH:(ts + 1) * CH],
                         ssum[0:1, ts * CH:(ts + 1) * CH], st_s[0:1, :])
                    p.op("vector", "tensor_add", ssq[0:1, ts * CH:(ts + 1) * CH],
                         ssq[0:1, ts * CH:(ts + 1) * CH], st_q[0:1, :], inc=s_dv)
                    stat_dv = s_dv.n
                return emit
            pending_stats.append(make_stats())
            p.w("scalar", s_dv, rb_mark)
            p.op("scalar", "dma_start", inc=SC[rrot], inck=16,
                 out=out_dram[co * 128:(co + 1) * 128, ts * CH:(ts + 1) * CH],
                 in_=r_f[rrot][:, :])
            r_free[("r", rrot)] = SC[rrot].n
            tile_i += 1
        co_done[co] = s_pe.n
    while pending_stats:
        pending_stats.pop(0)()


def emit_ln_finalize(p, sems, ssum, ssq, tmp1, tmp2, mh):
    s_ac, s_dv = sems["ac"], sems["dv"]
    inv = 1.0 / C
    p.op("vector", "tensor_scalar", tmp1[:, :], ssum[:, :], inv, None, ALU.mult)
    p.op("vector", "tensor_copy", mh["mh"][:, :], tmp1[:, :])
    p.op("vector", "tensor_copy", tmp2[:, :], mh["mh"][:, :])
    p.op("vector", "tensor_sub", tmp2[:, :], tmp1[:, :], tmp2[:, :])
    p.op("vector", "tensor_copy", mh["ml"][:, :], tmp2[:, :])
    p.op("vector", "tensor_mul", tmp2[:, :], tmp1[:, :], tmp1[:, :])
    p.op("vector", "tensor_scalar", tmp1[:, :], ssq[:, :], inv, None, ALU.mult)
    p.op("vector", "tensor_sub", tmp1[:, :], tmp1[:, :], tmp2[:, :])
    p.op("vector", "tensor_scalar", tmp1[:, :], tmp1[:, :], 1e-5, None, ALU.add,
         inc=s_dv)
    vm = s_dv.n
    p.w("scalar", s_dv, vm)
    p.op("scalar", "sqrt", tmp2[:, :], tmp1[:, :], inc=s_ac)
    p.w("vector", s_ac, s_ac.n)
    p.op("vector", "reciprocal", tmp1[:, :], tmp2[:, :], inc=s_dv)
    p.w("vector", s_dv, s_dv.n)
    p.op("vector", "tensor_copy", mh["rh"][:, :], tmp1[:, :])
    p.op("vector", "tensor_copy", tmp2[:, :], mh["rh"][:, :])
    p.op("vector", "tensor_sub", tmp2[:, :], tmp1[:, :], tmp2[:, :])
    p.op("vector", "tensor_copy", mh["rl"][:, :], tmp2[:, :], inc=s_dv)


def emit_ln_apply(p, sems, r_dram, mh, scratch_row, M_sb, R_sb, rr_f, yy_f,
                  br_ps, ones_r, *, out_dram, y_b, ysp, ysum, out_dram2, OUT):
    s_in, s_pe = sems["in"], sems["pe"]
    s_ac, s_dv, s_st = sems["ac"], sems["dv"], sems["st"]
    LD = [sems["ld%d" % i] for i in range(4)]
    SC = [sems["sc%d" % i] for i in range(4)]
    hl_mark = s_dv.n
    r_marks = {}
    buf_free = {}
    mr_used = {}
    mr_mark = {}
    tile_i = 0
    for ts in range(4):
        p.w("tensor", s_dv, hl_mark)
        p.w("tensor", s_ac, mr_mark.get(ts - 1, 0))
        sl = slice(ts * CH, (ts + 1) * CH)
        p.op("tensor", "matmul", br_ps[:, 0:512], ones_r[:, :], mh["mh"][0:1, sl],
             start=True, stop=False)
        p.op("tensor", "matmul", br_ps[:, 0:512], ones_r[:, :], mh["ml"][0:1, sl],
             start=False, stop=True)
        p.op("tensor", "matmul", br_ps[:, 512:1024], ones_r[:, :], mh["rh"][0:1, sl],
             start=True, stop=False)
        p.op("tensor", "matmul", br_ps[:, 512:1024], ones_r[:, :], mh["rl"][0:1, sl],
             start=False, stop=True, inc=s_pe)
        br_mark = s_pe.n
        p.w("scalar", s_pe, br_mark)
        p.w("scalar", s_dv, mr_used.get(ts - 2, 0))
        p.op("scalar", "copy", M_sb[ts % 2][:, :], br_ps[:, 0:512])
        p.op("scalar", "copy", R_sb[ts % 2][:, :], br_ps[:, 512:1024], inc=s_ac)
        mr_mark[ts] = s_ac.n
        p.w("vector", s_ac, mr_mark[ts])
        for fi in range(NFI):
            rrot = tile_i % 3
            if tile_i >= 3:
                p.w("sync", s_dv, buf_free.get(("rr", rrot), 0))
            p.op("sync", "dma_start", inc=LD[rrot], inck=16,
                 out=rr_f[rrot][:, :], in_=r_dram[fi * 128:(fi + 1) * 128, sl])
            r_marks[tile_i] = LD[rrot].n
            p.w("vector", LD[rrot], r_marks[tile_i])
            yrot = tile_i % 3
            if tile_i >= 3:
                p.w("vector", SC[yrot], buf_free.get(("yy", yrot), 0))
            p.op("vector", "tensor_sub", yy_f[yrot][:, :], rr_f[rrot][:, :],
                 M_sb[ts % 2][:, :])
            p.op("vector", "tensor_mul", yy_f[yrot][:, :], yy_f[yrot][:, :],
                 R_sb[ts % 2][:, :], inc=s_dv)
            buf_free[("rr", rrot)] = s_dv.n
            y_mark = s_dv.n
            if y_b is not None:
                p.op("vector", "tensor_copy",
                     y_b[:, fi * TOK + ts * CH: fi * TOK + (ts + 1) * CH],
                     yy_f[yrot][:, :])
                p.op("vector", "tensor_reduce", ysum[:, fi * 4 + ts: fi * 4 + ts + 1],
                     yy_f[yrot][:, :], AX.X, ALU.add, inc=s_dv)
                y_mark = s_dv.n
            p.w("scalar", s_dv, y_mark)
            dst = OUT if OUT is not None else out_dram
            p.op("scalar", "dma_start", inc=SC[yrot], inck=16,
                 out=dst[fi * 128:(fi + 1) * 128, sl], in_=yy_f[yrot][:, :])
            buf_free[("yy", yrot)] = SC[yrot].n
            tile_i += 1
        mr_used[ts] = s_dv.n


def emit_coeffs_pre(p, sems, ysum, wqtf_b, tiles, cps):
    s_in = sems["in"]
    s_pe, s_dv, s_cc = sems["pe"], sems["dv"], sems["cc"]
    ysb, q4, q_sb = tiles[0], tiles[1], tiles[2]
    ccin, ccout = tiles[-2], tiles[-1]
    qps = cps[:, 0:4]
    p.op("vector", "tensor_copy", ysb[:, :], ysum[:, :], inc=s_dv)
    m = s_dv.n
    p.w("tensor", s_dv, m)
    for c in range(NFI):
        p.op("tensor", "matmul", qps, wqtf_b[:, c * 128:(c + 1) * 128],
             ysb[:, c * 4:(c + 1) * 4], start=(c == 0), stop=(c == NFI - 1),
             inc=s_pe if c == NFI - 1 else None)
    p.w("vector", s_pe, s_pe.n)
    p.op("vector", "tensor_copy", q4[:, :], qps, inc=s_dv)
    p.w("vector", s_dv, s_dv.n)
    p.op("vector", "tensor_reduce", q_sb[:, :], q4[:, :], AX.X, ALU.add, inc=s_dv)
    p.w("scalar", s_dv, s_dv.n)
    p.op("scalar", "dma_start", inc=s_in, inck=16,
         out=ccin.ap().rearrange("a k -> k a"), in_=q_sb[:, :])
    p.w("gpsimd", s_in, s_in.n)
    p.op("gpsimd", "collective_compute", "AllGather", ALU.bypass,
         replica_groups=[[0, 1], [2, 3], [4, 5], [6, 7]],
         ins=[ccin.ap()], outs=[ccout.ap()], inc=s_cc)


def emit_coeffs_post(p, sems, keysT_b, i8_b, ind_b, ones_1, tiles, c_bc, cps):
    s_in = sems["in"]
    s_pe, s_ac, s_dv, s_cc = sems["pe"], sems["ac"], sems["dv"], sems["cc"]
    (ysb, q4, q_sb, qa, qb, qt, qt_bf, qc_bf, sc_bf,
     e8, se, rse, cf, cf_bf, ct_bf, ccin, ccout) = tiles
    qcps = cps[:, 4:5]
    scps = cps[0:8, 8:9]
    sctps = cps[0:1, 12:20]
    ctps = cps[0:8, 24:25]
    cbps = cps[:, 28:32]
    p.w("sync", s_cc, sems["cc"].n)
    p.op("sync", "dma_start", inc=s_in, inck=16, out=qa[:, :], in_=ccout.ap()[0:1, :])
    p.op("sync", "dma_start", inc=s_in, inck=16, out=qb[:, :], in_=ccout.ap()[1:2, :])
    p.w("vector", s_in, s_in.n)
    p.op("vector", "tensor_add", qt[:, :], qa[:, :], qb[:, :])
    p.op("vector", "tensor_copy", qt_bf[:, :], qt[:, :], inc=s_dv)
    p.w("tensor", s_dv, s_dv.n)
    p.op("tensor", "matmul", qcps, qt_bf[:, :], ones_1[:, :],
         start=True, stop=True, inc=s_pe)
    p.w("vector", s_pe, s_pe.n)
    p.op("vector", "tensor_copy", qc_bf[:, :], qcps, inc=s_dv)
    p.w("tensor", s_dv, s_dv.n)
    p.op("tensor", "matmul", scps, keysT_b[:, :], qc_bf[:, :],
         start=True, stop=True, inc=s_pe)
    p.w("vector", s_pe, s_pe.n)
    p.op("vector", "tensor_copy", sc_bf[:, :], scps, inc=s_dv)
    p.w("tensor", s_dv, s_dv.n)
    p.op("tensor", "matmul", sctps, sc_bf[:, :], i8_b[:, :],
         start=True, stop=True, inc=s_pe)
    p.w("scalar", s_pe, s_pe.n)
    p.op("scalar", "activation", e8[:, :], sctps, AF.Exp,
         scale=float(1.0 / (T * np.sqrt(128.0))), accum_out=se[:, :], inc=s_ac)
    p.w("vector", s_ac, s_ac.n)
    p.op("vector", "reciprocal", rse[:, :], se[:, :], inc=s_dv)
    p.w("scalar", s_dv, s_dv.n)
    p.op("scalar", "activation", cf[:, :], e8[:, :], AF.Copy,
         scale=rse[0:1, 0:1], inc=s_ac)
    p.w("vector", s_ac, s_ac.n)
    p.op("vector", "tensor_copy", cf_bf[:, :], cf[:, :], inc=s_dv)
    p.w("tensor", s_dv, s_dv.n)
    p.op("tensor", "matmul", ctps, cf_bf[:, :], ones_1[:, :],
         start=True, stop=True, inc=s_pe)
    p.w("vector", s_pe, s_pe.n)
    p.op("vector", "tensor_copy", ct_bf[:, :], ctps, inc=s_dv)
    p.w("tensor", s_dv, s_dv.n)
    for ecx in range(4):
        p.op("tensor", "matmul", cbps[:, ecx:ecx + 1],
             ind_b[:, ecx * 128:(ecx + 1) * 128],
             ct_bf[:, :], start=True, stop=True,
             inc=s_pe if ecx == 3 else None)
    p.w("vector", s_pe, s_pe.n)
    p.op("vector", "tensor_copy", c_bc[:, 0:4], cbps, inc=s_dv)



def _perm128():
    return np.concatenate([np.arange(0, 128, 2), np.arange(1, 128, 2)])


def _build_host_constants():
    pm = _perm128()
    j = np.arange(64)
    freqs = 1.0 / (10000.0 ** (2 * j / 128.0))
    ang = np.outer(freqs, np.arange(T))
    cosk = np.cos(ang).astype(np.float32)
    sink = np.sin(ang).astype(np.float32)
    i8 = np.eye(8, dtype=np.float32)
    ind = (np.arange(512)[None, :] // 64 == np.arange(8)[:, None]).astype(np.float32)
    cst = np.zeros((128, 8), np.float32)
    cst[:, 0] = EXP_SHIFT
    masks = {}
    cosq = {}
    sinq = {}
    for role in (0, 1):
        own = ROLE_CHUNKS[role]
        m = np.zeros((128, 4 * 2 * 4 * CH), np.float32)
        for s in range(4):
            cs = CCOUNT[s]
            cq = own[s]
            for which in range(2):
                ck = cs - 2 + which
                for ksub in range(4):
                    base = ((s * 2 + which) * 4 + ksub) * CH
                    kglob = ck * CH + ksub * 128 + np.arange(128)[:, None]
                    qglob = cq * CH + np.arange(CH)[None, :]
                    m[:, base:base + CH] = (kglob <= qglob)
        masks[role] = m
        own_cols = np.concatenate([np.arange(cq * CH, (cq + 1) * CH) for cq in own])
        cosq[role] = np.ascontiguousarray(cosk[:, own_cols])
        sinq[role] = np.ascontiguousarray(sink[:, own_cols])
    return pm, cosk, sink, i8, ind, cst, masks, cosq, sinq


def kernel(**inputs):
    x = np.asarray(inputs["L0_attn_x"], dtype=np.float32)
    Wq = np.asarray(inputs["L0_attn_Wq"], dtype=np.float32)
    Wk = np.asarray(inputs["L0_attn_Wk"], dtype=np.float32)
    Wv = np.asarray(inputs["L0_attn_Wv"], dtype=np.float32)
    Wo = np.asarray(inputs["L0_attn_Wo"], dtype=np.float32)
    bo = np.asarray(inputs["L0_attn_bo"], dtype=np.float32)
    Worg = np.asarray(inputs["L0_ffn_W_org"], dtype=np.float32)
    ld = np.asarray(inputs["L0_ffn_lora_down"], dtype=np.float32)
    lu = np.asarray(inputs["L0_ffn_lora_up"], dtype=np.float32)
    lk = np.asarray(inputs["L0_ffn_lora_keys"], dtype=np.float32)
    fWq = np.asarray(inputs["L0_ffn_Wq"], dtype=np.float32)

    pm, cosk, sink, i8, ind, cst, masks, cosq, sinq = _build_host_constants()
    perm_q = np.concatenate([h * 128 + pm for h in range(H)])
    perm_k = np.concatenate([g * 128 + pm for g in range(G)])
    wq_p = np.ascontiguousarray(Wq[:, perm_q])
    wk_p = np.ascontiguousarray(Wk[:, perm_k])
    wv_c = np.ascontiguousarray(Wv)
    wo_c = np.ascontiguousarray(Wo)
    worgT = np.ascontiguousarray(Worg.T)
    down_all = np.ascontiguousarray(ld.transpose(1, 0, 2).reshape(C, 512))
    up_all = np.ascontiguousarray(lu.reshape(512, C))
    wqtf = np.ascontiguousarray(fWq.T)
    keysT = np.ascontiguousarray(lk.T)
    bo_t = np.ascontiguousarray(bo.reshape(NFI, 128).T)

    xTb = [np.ascontiguousarray(x[b].T) for b in range(B)]
    in_maps = []
    for core in range(8):
        b, role = core // 2, core % 2
        own = ROLE_CHUNKS[role]
        own_cols = np.concatenate([np.arange(cq * CH, (cq + 1) * CH) for cq in own])
        in_maps.append({
            "xT_own": np.ascontiguousarray(xTb[b][:, own_cols]),
            "xT_full": xTb[b],
            "wq": wq_p, "wk": wk_p, "wv": wv_c, "wo": wo_c, "bo_t": bo_t,
            "worgT": worgT, "down": down_all, "up": up_all, "wqtf": wqtf,
            "keysT": keysT, "i8": i8, "ind": ind,
            "cosk": cosk, "sink": sink,
            "cosq": cosq[role], "sinq": sinq[role],
            "maskf": masks[role], "cst": cst,
        })

    nc = build_program()
    res = run_bass_kernel_spmd(nc, in_maps, core_ids=list(range(8)))

    out = np.empty((B, T, C), np.float32)
    for core in range(8):
        b, role = core // 2, core % 2
        rT = np.asarray(res.results[core]["OUT"]).T
        for s, cq in enumerate(ROLE_CHUNKS[role]):
            out[b, cq * CH:(cq + 1) * CH, :] = rT[s * CH:(s + 1) * CH, :]
    if DBG["stage"] is not None:
        kernel._dbg = res
    return out
